# version 01
# baseline (speedup 1.0000x reference)
import numpy as np
import ml_dtypes
import concourse.bass as bass
import concourse.mybir as mybir
from concourse.bass_utils import run_bass_kernel_spmd
from concourse.alu_op_type import AluOpType as ALU
from bass_rust import ReduceOp
import contextlib, os
from concourse import library_config
_KMAX = int(os.environ.get("KMAX", "99"))

F32 = mybir.dt.float32
BF16 = mybir.dt.bfloat16
U8 = mybir.dt.uint8
I32 = mybir.dt.int32
AF = mybir.ActivationFunctionType
AX = mybir.AxisListType.X

B, T, D = 2, 8192, 1024
NCORE = 8
TPC = T * B // NCORE
NT = TPC // 128
HEADS0, HD0 = 16, 64
HEADS1, HD1, FEAT, WS = 8, 128, 64, 7
EPS_LN = 1e-5

C_HALF = 3.0 * 2.0**21
C_ONE = 3.0 * 2.0**22


class Ctx:

    def __init__(self, nc, stack):
        self.nc = nc
        self.stack = stack
        self.sems = {}
        self.cnt = {}

    def sem(self, name):
        if name not in self.sems:
            self.sems[name] = self.nc.alloc_semaphore(name=name)
            self.cnt[name] = 0
        return self.sems[name]

    def free(self, names):
        hs = [self.sems.pop(n) for n in names if n in self.sems]
        for n in names:
            self.cnt.pop(n, None)
        self.nc.clear_and_free_semaphores(hs)
        self.nc.all_engine_barrier()

    def inc(self, instr, name, n=1):
        instr.then_inc(self.sem(name), n)
        self.cnt[name] += n
        return self.cnt[name]

    def wait(self, eng, name, val):
        eng.wait_ge(self.sem(name), val)


def emit_store(ctx, nc, dout, resid, ST):
    with nc.Block() as b:
        @b.gpsimd
        def _(e):
            ctx.inc(e.dma_start(out=dout.rearrange("(t p) d -> p t d", p=128), in_=resid[:]), ST, 16)
            e.wait_ge(ctx.sems[ST], ctx.cnt[ST])


def build_nc(core_id):
    nc = bass.Bass("TRN2", target_bir_lowering=False, detect_race_conditions=False)
    stack = contextlib.ExitStack()
    ctx = Ctx(nc, stack)

    dx = nc.dram_tensor("xs", [TPC, D], F32, kind="ExternalInput").ap()
    dout = nc.dram_tensor("out", [TPC, D], F32, kind="ExternalOutput").ap()
    ddbg = nc.dram_tensor("dbg", [128, 4 * D], F32, kind="ExternalOutput").ap()
    d_w0 = nc.dram_tensor("L0_ffn_weight", [D, D], F32, kind="ExternalInput").ap()
    d_consts = nc.dram_tensor("scalars_c", [128, 8], F32, kind="ExternalInput").ap()
    d_bias0 = nc.dram_tensor("bias0_b", [128, D], F32, kind="ExternalInput").ap()
    d_iden = nc.dram_tensor("iden", [128, 128], BF16, kind="ExternalInput").ap()
    d_ones128 = nc.dram_tensor("ones128", [128, 128], F32, kind="ExternalInput").ap()
    d_wqkv0 = nc.dram_tensor("L0_attn_w_qkv", [3 * D, D], F32, kind="ExternalInput").ap()
    d_wout0 = nc.dram_tensor("L0_attn_w_out", [D, D], F32, kind="ExternalInput").ap()
    d_ck = nc.dram_tensor("rope_cos", [128, NT * 32], F32, kind="ExternalInput").ap()
    d_sk = nc.dram_tensor("rope_sin", [128, NT * 32], F32, kind="ExternalInput").ap()
    cc_kv_in = nc.dram_tensor("cc_kv_in", [64, 16, 65], F32)
    cc_kv_out = nc.dram_tensor("cc_kv_out", [64, 16, 65], F32)

    def sb(name, shape, dt=F32):
        return stack.enter_context(nc.sbuf_tensor(name, shape, dt))

    resid = sb("resid", [128, NT, D])
    halfbias = sb("halfbias", [128, D])
    iden = sb("iden_s", [128, 128], BF16)
    consts = sb("consts", [128, 8])
    nhalfalpha = sb("nhalfalpha", [128, 1])
    mega = sb("mega", [128, 34688])
    mg = mega[:]
    mgb = mg.bitcast(BF16)

    def mgf32(off_kb, n_f32):
        return mg[:, int(off_kb * 256):int(off_kb * 256) + n_f32]

    def mgbf(off_kb, n_bf):
        return mgb[:, int(off_kb * 512):int(off_kb * 512) + n_bf]

    ternT = mgbf(48, 8192).rearrange("p (c d) -> p c d", d=D)
    wtmp = mgf32(0, 8192).rearrange("p (c d) -> p c d", d=D)
    terng = mgbf(32, 8192).rearrange("p (c d) -> p c d", d=D)
    wqkvT = mgbf(0, 24576).rearrange("p (k n) -> p k n", k=8)
    woutT = mgbf(0, 8192).rearrange("p (k n) -> p k n", k=8)
    ones128 = mgf32(64, 128)
    qfeat = mgbf(64, 16384).rearrange("p (pr n) -> p pr n", pr=8)
    wstage = mgbf(96, 8192).rearrange("p (c d) -> p c d", d=D)
    qrot = mgf32(96, 1024)
    e_r = mgf32(100, 1024)
    e_m = mgf32(104, 1024)
    ropeA = mgf32(108, 256)
    ropeB = mgf32(109, 256)
    ropeC = mgf32(110, 256)
    ropeD = mgf32(111, 256)
    attn_tok = mgbf(96, 1024)
    afm = mgbf(98, 1024)
    xln_bf = mgbf(112, 1024)
    xfm = mgbf(114, 1024)
    e_e = mgf32(116, 1024)
    tsc1t_v = mgf32(120, 1160)
    tsc1 = tsc1t_v[:, 0:1024]
    vaug = mgbf(120, 1280)
    kfeat = mgbf(122.5, 1024)
    qftok = mgbf(125, 1024)
    kv_acc = mg[0:64, int(127 * 256):int(127 * 256) + 1040]
    ropekv = mgf32(131.5, 1024)
    ck = ropekv[:][:, 0:512]
    sk = ropekv[:][:, 512:1024]
    kv_pair = ropekv[:].bitcast(BF16)[:, 0:1280].rearrange("p (pr n) -> p pr n", n=160)
    lnS = sb("lnS", [128, 128])
    lnQ = sb("lnQ", [128, 128])
    lnM = sb("lnM", [128, 128])
    lnV = sb("lnV", [128, 128])
    lnR = sb("lnR", [128, 128])
    lnRn = sb("lnRn", [128, 128])
    lnB = sb("lnB", [128, 128])
    dall = lnS[:][:, 0:16]
    rall = lnS[:][:, 16:32]
    asum = lnV[:][:, 0:16]
    acol = lnM[:][:, 0:1]
    atot = lnM[:][:, 1:2]
    delta = lnM[:][:, 2:3]
    ndelta = lnM[:][:, 3:4]

    def ps(name, shape, dt=F32):
        return stack.enter_context(nc.psum_tensor(name, shape, dt))

    ps_tr = ps("ps_tr", [128, D], BF16)
    psA = [ps(f"psA{i}", [128, 512]) for i in range(6)]
    ps_mm0, ps_mm1 = psA[0], psA[1]
    ps_red = psA[2][:, 0:1]
    ps_qa, ps_qb = psA[0], psA[1]
    ps_kv = [psA[2 + i][0:64, 0:260] for i in range(4)]
    ps_qkv = [psA[2 + i][:, 0:320] for i in range(4)]

    LD = "ld"
    ST = "st"

    with nc.Block() as b:
        @b.sync
        def _(e):
            ctx.inc(e.dma_start(out=resid[:], in_=dx.rearrange("(t p) d -> p t d", p=128)), LD, 16)
            ctx.inc(e.dma_start(out=halfbias[:], in_=d_bias0), LD, 16)
            ctx.inc(e.dma_start(out=iden[:], in_=d_iden), LD, 16)
            ctx.inc(e.dma_start(out=ones128, in_=d_ones128), LD, 16)
            ctx.inc(e.dma_start(out=consts[:], in_=d_consts), LD, 16)
            ctx.inc(e.dma_start(out=wtmp, in_=d_w0.rearrange("(c p) d -> p c d", p=128)), LD, 16)

    with nc.Block() as b:
        @b.gpsimd
        def _(e):
            e.memset(lnS[:], 0.0)
            e.memset(lnQ[:], 0.0)
            e.memset(kv_acc[:], 0.0)
            e.memset(mega[:, 27648:28928], 0.0)

    with nc.Block() as b:
        SQ = "p1s"
        VQ = "p1v"
        GQ = "p1g"
        TQ = "p1t"

        @b.scalar
        def _(e):
            ctx.wait(e, LD, ctx.cnt[LD])
            for c in range(8):
                for hh in range(2):
                    i = ctx.inc(e.activation(out=xln_bf[:].bitcast(F32)[:, 0:512],
                                             in_=wtmp[:, c, 512 * hh:512 * (hh + 1)], func=AF.Abs,
                                             accum_out=asum[:, 2 * c + hh:2 * c + hh + 1]), SQ)

        @b.vector
        def _(e):
            ctx.wait(e, SQ, 16)
            ctx.inc(e.tensor_reduce(out=acol, in_=asum, axis=AX, op=ALU.add), VQ)
            ctx.wait(e, TQ, 1)
            ctx.inc(e.tensor_copy(atot, ps_red), VQ)
            e.drain()
            e.tensor_scalar(out=delta, in0=atot, scalar1=0.7 / (D * D), scalar2=None, op0=ALU.mult)
            e.tensor_scalar(out=ndelta, in0=atot, scalar1=-0.7 / (D * D), scalar2=None, op0=ALU.mult)
            e.tensor_scalar(out=nhalfalpha[:], in0=consts[:, 0:1], scalar1=-0.5, scalar2=None, op0=ALU.mult)
            e.tensor_scalar(out=halfbias[:], in0=halfbias[:], scalar1=0.5, scalar2=None, op0=ALU.mult)
            e.drain()
            for c in range(8):
                e.tensor_scalar(out=tsc1, in0=wtmp[:, c, :], scalar1=delta, scalar2=None, op0=ALU.is_gt)
                ctx.inc(e.scalar_tensor_tensor(out=terng[:, c, :], in0=wtmp[:, c, :], scalar=ndelta,
                                               in1=tsc1, op0=ALU.is_lt, op1=ALU.subtract), VQ)
            for k in range(8):
                ctx.wait(e, TQ, 1 + 8 * (k + 1))
                ctx.inc(e.tensor_copy(ternT[:, k, :], ps_tr[:]), VQ)

        @b.gpsimd
        def _(e):
            pass

        @b.tensor
        def _(e):
            ctx.wait(e, VQ, 1)
            ctx.inc(e.matmul(ps_red, ones128, acol, start=True, stop=True), TQ)
            ctx.wait(e, VQ, 10)
            for k in range(8):
                if k > 0:
                    ctx.wait(e, VQ, 10 + k)
                for o in range(8):
                    ctx.inc(e.transpose(ps_tr[:, 128 * o:128 * (o + 1)],
                                        terng[:, o, 128 * k:128 * (k + 1)], iden[:]), TQ)

    def ln_prepass_v(e, base):
        for t in range(NT):
            e.tensor_reduce(out=lnS[:, t:t + 1], in_=resid[:, t, :], axis=AX, op=ALU.add)
        ctx.wait(e, "lnp_s", base["lnp_s"] + 16)
        e.drain()
        e.tensor_scalar(out=lnM[:], in0=lnS[:], scalar1=1.0 / D, scalar2=None, op0=ALU.mult)
        e.tensor_tensor(out=lnV[:], in0=lnM[:], in1=lnM[:], op=ALU.mult)
        e.scalar_tensor_tensor(out=lnV[:], in0=lnQ[:], scalar=1.0 / D, in1=lnV[:],
                               op0=ALU.mult, op1=ALU.subtract)
        e.tensor_scalar(out=lnV[:], in0=lnV[:], scalar1=EPS_LN, scalar2=None, op0=ALU.add)
        ctx.inc(e.reciprocal(lnV[:], lnV[:]), "lnp_v")
        ctx.wait(e, "lnp_s2", base["lnp_s2"] + 1)
        ctx.inc(e.tensor_tensor(out=lnB[:], in0=lnM[:], in1=lnRn[:], op=ALU.mult), "lnp_v3")

    def ln_prepass_s(e, base):
        for t in range(NT):
            i = e.activation(out=xln_bf[:], in_=resid[:, t, :], func=AF.Square,
                             accum_out=lnQ[:, t:t + 1])
        ctx.inc(i, "lnp_s", 16)
        ctx.wait(e, "lnp_v", base["lnp_v"] + 1)
        e.activation(out=lnR[:], in_=lnV[:], func=AF.Sqrt)
        ctx.inc(e.mul(lnRn[:], lnR[:], -1.0), "lnp_s2")

    def snap(*names):
        return {n: ctx.cnt.get(n, 0) for n in names}

    LNSEMS = ("lnp_s", "lnp_v", "lnp_s2", "lnp_v3")
    for n in LNSEMS:
        ctx.sem(n)

    def emit_l0ffn(tag):
        SAP = tag + "sap"
        TTR = tag + "ttr"
        VEV = tag + "vev"
        TMM = tag + "tmm"
        VEP = tag + "vep"
        base = snap(*LNSEMS)

        with nc.Block() as b:
            @b.vector
            def _(e):
                ln_prepass_v(e, base)
                for t in range(NT):
                    ctx.wait(e, TTR, 8 * (t + 1))
                    ctx.inc(e.tensor_copy(xfm[:], ps_tr[:]), VEV)
                    ctx.wait(e, TMM, 2 * (t + 1))
                    for n in range(2):
                        pm = ps_mm0 if n == 0 else ps_mm1
                        e.scalar_tensor_tensor(out=tsc1[:, 512 * n:512 * (n + 1)], in0=pm[:],
                                               scalar=nhalfalpha[:], in1=halfbias[:, 512 * n:512 * (n + 1)],
                                               op0=ALU.mult, op1=ALU.add)
                    e.tensor_tensor(out=resid[:, t, 0:512], in0=tsc1[:, 0:512],
                                    in1=resid[:, t, 0:512], op=ALU.add)
                    i = e.tensor_tensor(out=resid[:, t, 512:1024], in0=tsc1[:, 512:1024],
                                        in1=resid[:, t, 512:1024], op=ALU.add)
                    ctx.inc(i, VEP)

            @b.scalar
            def _(e):
                ln_prepass_s(e, base)
                ctx.wait(e, "lnp_v3", base["lnp_v3"] + 1)
                for t in range(NT):
                    if t > 0:
                        ctx.wait(e, TTR, 8 * t)
                    ctx.inc(e.activation(out=xln_bf[:], in_=resid[:, t, :], func=AF.Identity,
                                         scale=lnR[:, t:t + 1], bias=lnB[:, t:t + 1]), SAP)

            @b.tensor
            def _(e):
                for t in range(NT):
                    ctx.wait(e, SAP, t + 1)
                    if t > 0:
                        ctx.wait(e, VEV, t)
                    for k in range(8):
                        ctx.inc(e.transpose(ps_tr[:, 128 * k:128 * (k + 1)],
                                            xln_bf[:, 128 * k:128 * (k + 1)], iden[:]), TTR)
                    ctx.wait(e, VEV, t + 1)
                    if t > 0:
                        ctx.wait(e, VEP, t)
                    for n in range(2):
                        pm = ps_mm0 if n == 0 else ps_mm1
                        for k in range(8):
                            i = e.matmul(pm[:], xfm[:, 128 * k:128 * (k + 1)],
                                         ternT[:, k, 512 * n:512 * (n + 1)],
                                         start=(k == 0), stop=(k == 7))
                        ctx.inc(i, TMM)

    emit_l0ffn("f0")

    if _KMAX < 3:
        emit_store(ctx, nc, dout, resid, ST)
        stack.close()
        return nc
    with nc.Block() as b:
        W3 = "p3w"
        T3 = "p3t"
        V3 = "p3v"

        @b.gpsimd
        def _(e):
            for g in range(3):
                ctx.inc(e.dma_start(
                    out=wstage,
                    in_=d_wqkv0[1024 * g:1024 * (g + 1), :].rearrange("(c p) d -> p c d", p=128)), W3, 16)
                ctx.wait(e, T3, 64 * (g + 1))
            ctx.inc(e.dma_start(out=ck, in_=d_ck), W3, 16)
            ctx.inc(e.dma_start(out=sk, in_=d_sk), W3, 16)


        @b.tensor
        def _(e):
            for g in range(3):
                ctx.wait(e, W3, 16 * (g + 1))
                for k in range(8):
                    if not (g == 0 and k == 0):
                        ctx.wait(e, V3, 8 * g + k)
                    for o in range(8):
                        ctx.inc(e.transpose(ps_tr[:, 128 * o:128 * (o + 1)],
                                            wstage[:, o, 128 * k:128 * (k + 1)], iden[:]), T3)

        @b.vector
        def _(e):
            for g in range(3):
                for k in range(8):
                    ctx.wait(e, T3, 64 * g + 8 * (k + 1))
                    ctx.inc(e.tensor_copy(wqkvT[:, k, 1024 * g:1024 * (g + 1)], ps_tr[:]), V3)

    if _KMAX < 4:
        emit_store(ctx, nc, dout, resid, ST)
        stack.close()
        return nc
    with nc.Block() as b:
        base = snap(*LNSEMS)
        SAP = "a1sap"
        TTR = "a1ttr"
        VEV = "a1vev"
        TQK = "a1tqk"
        VRE = "a1vre"
        SEX = "a1sex"
        TPF = "a1tpf"
        VQF = "a1vqf"
        TKV = "a1tkv"

        @b.scalar
        def _(e):
            ln_prepass_s(e, base)
            ctx.wait(e, "lnp_v3", base["lnp_v3"] + 1)
            for t in range(NT):
                if t > 0:
                    ctx.wait(e, TTR, 8 * t)
                ctx.inc(e.activation(out=xln_bf[:], in_=resid[:, t, :], func=AF.Identity,
                                     scale=lnR[:, t:t + 1], bias=lnB[:, t:t + 1]), SAP)
                ctx.wait(e, VRE, 6 * t + 1)
                ctx.inc(e.activation(out=e_e[:], in_=e_m, func=AF.Exp), SEX)
                ctx.wait(e, VRE, 6 * t + 3)
                ctx.inc(e.activation(out=e_e[:], in_=e_m, func=AF.Exp), SEX)

        @b.tensor
        def _(e):
            for t in range(NT):
                ctx.wait(e, SAP, t + 1)
                if t > 0:
                    ctx.wait(e, VQF, t)
                for k in range(8):
                    ctx.inc(e.transpose(ps_tr[:, 128 * k:128 * (k + 1)],
                                        xln_bf[:, 128 * k:128 * (k + 1)], iden[:]), TTR)
                ctx.wait(e, VEV, t + 1)
                for sec in range(6):
                    pm = ps_qa if sec % 2 == 0 else ps_qb
                    w = 6 * t + sec - 1
                    if w > 0:
                        ctx.wait(e, VRE, w)
                    for k in range(8):
                        i = e.matmul(pm[:], xfm[:, 128 * k:128 * (k + 1)],
                                     wqkvT[:, k, 512 * sec:512 * (sec + 1)],
                                     start=(k == 0), stop=(k == 7))
                    ctx.inc(i, TQK)
                ctx.wait(e, VRE, 6 * t + 5)
                if t > 0:
                    ctx.wait(e, VQF, t)
                for pr in range(8):
                    ctx.inc(e.transpose(ps_tr[:, 128 * pr:128 * (pr + 1)],
                                        qftok[:, 128 * pr:128 * (pr + 1)], iden[:]), TPF)
                ctx.wait(e, VRE, 6 * t + 6)
                if t > 0:
                    ctx.wait(e, "a1vkv", 4 * t)
                for h in range(16):
                    i = e.matmul(ps_kv[h // 4][:, 65 * (h % 4):65 * (h % 4) + 65],
                                 kfeat[:, 64 * h:64 * (h + 1)],
                                 vaug[:, 80 * h:80 * h + 65],
                                 start=True, stop=True)
                    ctx.inc(i, TKV)

        @b.vector
        def _(e):
            e.tensor_scalar(out=vaug.rearrange("p (h n) -> p h n", n=80)[:, :, 64:65].squeeze(),
                            in0=consts[:, 4:5].broadcast_to([128, 16]),
                            scalar1=1.0, scalar2=None, op0=ALU.mult)
            ln_prepass_v(e, base)
            for t in range(NT):
                ctx.wait(e, TTR, 8 * (t + 1))
                ctx.inc(e.tensor_copy(xfm[:], ps_tr[:]), VEV)
                ckt = ck[:, 32 * t:32 * (t + 1)].rearrange("p (o n) -> p o n", o=1).broadcast_to([128, 8, 32])
                skt = sk[:, 32 * t:32 * (t + 1)].rearrange("p (o n) -> p o n", o=1).broadcast_to([128, 8, 32])

                def rope(pm, dst):
                    xe = pm[:].rearrange("p (h two f) -> p h two f", two=2, f=32)[:, :, 0, :]
                    xo = pm[:].rearrange("p (h two f) -> p h two f", two=2, f=32)[:, :, 1, :]
                    de = dst.rearrange("p (h two f) -> p h two f", two=2, f=32)[:, :, 0, :]
                    do = dst.rearrange("p (h two f) -> p h two f", two=2, f=32)[:, :, 1, :]
                    rA = ropeA.rearrange("p (h f) -> p h f", f=32)
                    rB = ropeB.rearrange("p (h f) -> p h f", f=32)
                    rC = ropeC.rearrange("p (h f) -> p h f", f=32)
                    rD = ropeD.rearrange("p (h f) -> p h f", f=32)
                    e.tensor_tensor(out=rA, in0=xe, in1=ckt, op=ALU.mult)
                    e.tensor_tensor(out=rB, in0=xo, in1=skt, op=ALU.mult)
                    e.tensor_tensor(out=rC, in0=xe, in1=skt, op=ALU.mult)
                    e.tensor_tensor(out=rD, in0=xo, in1=ckt, op=ALU.mult)
                    e.tensor_tensor(out=de, in0=rA, in1=rB, op=ALU.subtract)
                    e.tensor_tensor(out=do, in0=rC, in1=rD, op=ALU.add)

                ctx.wait(e, TQK, 6 * t + 1)
                rope(ps_qa, qrot[:, 0:512])
                ctx.wait(e, TQK, 6 * t + 2)
                rope(ps_qb, qrot[:, 512:1024])
                e.tensor_scalar(out=e_r, in0=qrot, scalar1=0.125, scalar2=0.0, op0=ALU.mult, op1=ALU.max)
                ctx.inc(e.tensor_scalar(out=e_m, in0=qrot, scalar1=0.125, scalar2=0.0,
                                        op0=ALU.mult, op1=ALU.min), VRE)
                ctx.wait(e, SEX, 2 * t + 1)
                ctx.inc(e.tensor_tensor(out=qftok, in0=e_r, in1=e_e[:], op=ALU.add), VRE)
                ctx.wait(e, TQK, 6 * t + 3)
                rope(ps_qa, qrot[:, 0:512])
                ctx.wait(e, TQK, 6 * t + 4)
                rope(ps_qb, qrot[:, 512:1024])
                e.tensor_scalar(out=e_r, in0=qrot, scalar1=0.0, scalar2=None, op0=ALU.max)
                ctx.inc(e.tensor_scalar(out=e_m, in0=qrot, scalar1=0.0, scalar2=None, op0=ALU.min), VRE)
                ctx.wait(e, TQK, 6 * t + 5)
                va = vaug.rearrange("p (h n) -> p h n", n=80)
                ctx.inc(e.tensor_copy(va[:, 0:8, 0:64], ps_qa[:].rearrange("p (h f) -> p h f", f=64)), VRE)
                ctx.wait(e, TQK, 6 * t + 6)
                ctx.inc(e.tensor_copy(va[:, 8:16, 0:64], ps_qb[:].rearrange("p (h f) -> p h f", f=64)), VRE)
                ctx.wait(e, SEX, 2 * t + 2)
                ctx.inc(e.tensor_tensor(out=kfeat, in0=e_r, in1=e_e[:], op=ALU.add), VRE)
                ctx.wait(e, TPF, 8 * (t + 1))
                ctx.inc(e.tensor_copy(qfeat[:, :, 128 * t:128 * (t + 1)],
                                      ps_tr[:].rearrange("p (pr c) -> p pr c", c=128)), VQF)
                ctx.wait(e, TKV, 16 * (t + 1))
                for g in range(4):
                    ctx.inc(e.tensor_tensor(out=kv_acc[:, 260 * g:260 * (g + 1)],
                                            in0=ps_kv[g], in1=kv_acc[:, 260 * g:260 * (g + 1)],
                                            op=ALU.add), "a1vkv")

    if _KMAX < 5:
        emit_store(ctx, nc, dout, resid, ST)
        stack.close()
        return nc
    with nc.Block() as b:
        V5 = "p5v"
        G5 = "p5g"
        T5 = "p5t"
        W5 = "p5w"

        @b.vector
        def _(e):
            for k in range(8):
                ctx.wait(e, T5, 8 * (k + 1))
                ctx.inc(e.tensor_copy(woutT[:, k, :], ps_tr[:]), V5)

        @b.gpsimd
        def _(e):
            ctx.inc(e.dma_start(out=wstage,
                                in_=d_wout0.rearrange("(c p) d -> p c d", p=128)), W5, 16)
            e.memset(kv_pair, 0.0)
            ctx.inc(e.dma_start(out=cc_kv_in.ap().rearrange("r h n -> r (h n)"), in_=kv_acc[:]), G5, 16)
            ctx.wait(e, G5, 16)
            ctx.inc(e.collective_compute(
                "AllReduce", ALU.add,
                replica_groups=[[0, 1, 2, 3], [4, 5, 6, 7]],
                ins=[cc_kv_in.ap().opt()],
                outs=[cc_kv_out.ap().opt()]), "p5gc", 1)
            ctx.wait(e, "p5gc", 1)
            evens = cc_kv_out.ap().rearrange("r (pr eo) n -> r eo pr n", eo=2)
            ctx.inc(e.dma_start(out=kv_pair[0:64, :, 0:65], in_=evens[:, 0:1, :, :].squeeze()), G5, 16)
            ctx.inc(e.dma_start(out=kv_pair[64:128, :, 80:145], in_=evens[:, 1:2, :, :].squeeze()), G5, 16)

        @b.tensor
        def _(e):
            ctx.wait(e, W5, 16)
            for k in range(8):
                if k > 0:
                    ctx.wait(e, V5, k)
                for o in range(8):
                    ctx.inc(e.transpose(ps_tr[:, 128 * o:128 * (o + 1)],
                                        wstage[:, o, 128 * k:128 * (k + 1)], iden[:]), T5)

    if _KMAX < 6:
        emit_store(ctx, nc, dout, resid, ST)
        stack.close()
        return nc
    KP6 = int(os.environ.get("KP6", "3"))
    TPT6 = {0: 1, 1: 1, 2: 2, 3: 3}[KP6]
    with nc.Block() as b:
        T6 = "a2t"
        V6 = "a2v"
        S6 = "a2s"
        E6 = "a2e"

        @b.tensor
        def _(e):
            ctx.wait(e, "p5g", ctx.cnt["p5g"])
            for t in range(NT):
                if t > 0 and KP6 >= 1:
                    ctx.wait(e, S6, t)
                for pr in range(8):
                    i = e.matmul(ps_qkv[pr // 2][:, 160 * (pr % 2):160 * (pr % 2) + 145],
                                 qfeat[:, pr, 128 * t:128 * (t + 1)],
                                 kv_pair[:, pr, 0:145],
                                 start=True, stop=True)
                ctx.inc(i, T6)
                if KP6 < 2:
                    continue
                ctx.wait(e, S6, t + 1)
                if t > 0:
                    ctx.wait(e, V6, 2 * t)
                for pr in range(8):
                    i = e.transpose(ps_tr[:, 128 * pr:128 * (pr + 1)],
                                    attn_tok[:, 128 * pr:128 * (pr + 1)], iden[:])
                ctx.inc(i, T6)
                if KP6 < 3:
                    continue
                ctx.wait(e, V6, 2 * t + 2)
                if t > 0:
                    ctx.wait(e, E6, t)
                for n in range(2):
                    pm = ps_mm0 if n == 0 else ps_mm1
                    for k in range(8):
                        i = e.matmul(pm[:], afm[:, 128 * k:128 * (k + 1)],
                                     woutT[:, k, 512 * n:512 * (n + 1)],
                                     start=(k == 0), stop=(k == 7))
                ctx.inc(i, T6)

        @b.vector
        def _(e):
            if KP6 < 1:
                return
            for t in range(NT):
                ctx.wait(e, T6, TPT6 * t + 1)
                for g in range(4):
                    e.tensor_scalar(out=dall[:, 4 * g:4 * g + 2],
                                    in0=ps_qkv[g].rearrange("p (pp n) -> p pp n", n=160)[:, :, 64:65].squeeze(),
                                    scalar1=1e-6, scalar2=None, op0=ALU.max)
                    e.tensor_scalar(out=dall[:, 4 * g + 2:4 * g + 4],
                                    in0=ps_qkv[g].rearrange("p (pp n) -> p pp n", n=160)[:, :, 144:145].squeeze(),
                                    scalar1=1e-6, scalar2=None, op0=ALU.max)
                e.drain()
                e.reciprocal(rall, dall)
                ctx.inc(e.drain(), V6)
                if KP6 < 2:
                    ctx.inc(e.drain(), V6)
                    continue
                ctx.wait(e, T6, TPT6 * t + 2)
                ctx.inc(e.tensor_copy(afm, ps_tr[:]), V6)
                if KP6 < 3:
                    continue
                ctx.wait(e, T6, TPT6 * (t + 1))
                e.tensor_tensor(out=resid[:, t, 0:512], in0=ps_mm0[:], in1=resid[:, t, 0:512], op=ALU.add)
                i = e.tensor_tensor(out=resid[:, t, 512:1024], in0=ps_mm1[:], in1=resid[:, t, 512:1024], op=ALU.add)
                ctx.inc(i, E6)

        @b.scalar
        def _(e):
            if KP6 < 1:
                return
            for t in range(NT):
                ctx.wait(e, V6, 2 * t + 1)
                for h in range(16):
                    pr, eo = h // 2, h % 2
                    rc = 4 * (pr // 2) + 2 * eo + (pr % 2)
                    i = e.activation(out=attn_tok[:, 64 * h:64 * (h + 1)],
                                     in_=ps_qkv[pr // 2][:, 160 * (pr % 2) + 80 * eo:160 * (pr % 2) + 80 * eo + 64],
                                     func=AF.Identity, scale=rall[:, rc:rc + 1])
                ctx.inc(i, S6)

    if _KMAX < 7:
        emit_store(ctx, nc, dout, resid, ST)
        stack.close()
        return nc
    emit_l0ffn("f1")

    if _KMAX < 8:
        emit_store(ctx, nc, dout, resid, ST)
        stack.close()
        return nc

    C2Q = 3.0 * 2.0**22
    wgu = [mgbf(16 * b, 8192).rearrange("p (w k n) -> p w k n", w=2, k=8) for b in range(2)]
    wdq = [mgbf(32 + 8 * b, 4096).rearrange("p (k n) -> p k n", k=4) for b in range(2)]
    s_abs = mgf32(64, 2048)
    s_a = mgf32(72, 2048)
    s_q1 = mgf32(80, 2048)
    hid_buf = s_q1
    s_q2 = mgf32(88, 2048)
    qbf = mgbf(88, 2048)
    s_g2 = mgf32(96, 2048)
    sgn_b = mgbf(104, 2048)
    smq = mgf32(108, 1280)
    q_bm = smq[:, 0:256]; q_sc = smq[:, 256:512]; q_cm = smq[:, 512:768]
    q_t1 = smq[:, 768:1024]; q_rec = smq[:, 1024:1280]
    mskq = halfbias[:].bitcast(U8)[:, 0:2048]
    silu_buf2 = mgf32(113, 512)
    xq_fm = mgbf(115, 4096).rearrange("p (k n) -> p k n", k=8)
    hq_fm = mgbf(123, 2048).rearrange("p (k n) -> p k n", k=4)
    down_acc = mgf32(48, 4096)
    gsv = mgf32(134.5, 32)
    wstrip = mgf32(0, 4096)
    mxt = mgf32(108, 640)[0:128, 0:129]
    wqstage = mgbf(16, 4096)

    d_gate_s = nc.dram_tensor("gate_strip", [512, D], F32, kind="ExternalInput").ap()
    d_up_s = nc.dram_tensor("up_strip", [512, D], F32, kind="ExternalInput").ap()
    d_down_s = nc.dram_tensor("down_strip", [D, 512], F32, kind="ExternalInput").ap()
    cc_mx_in = nc.dram_tensor("cc_mx_in", [128, 16], F32)
    cc_mx_out = nc.dram_tensor("cc_mx_out", [128, 16], F32)
    mx_sc = nc.dram_tensor("mx_sc", [16], F32)
    wqg_in = nc.dram_tensor("wqg_in", [3, 128, 4096], BF16)
    wqg_out = nc.dram_tensor("wqg_out", [8, 3, 128, 4096], BF16)

    NBQ = 128

    for n in ("qV", "qS", "qG"):
        ctx.sem(n)

    def q_v(e, i, x_ap, gs_ap, gs6_ap):
        x3 = x_ap.rearrange("p (nb b) -> p nb b", b=16)
        e.tensor_reduce(out=q_bm[:, 0:128], in_=x3, axis=AX, op=ALU.max, apply_absolute_value=True)
        e.tensor_scalar(out=q_sc, in0=q_bm, scalar1=gs6_ap, scalar2=448.0, op0=ALU.mult, op1=ALU.min)
        e.tensor_scalar(out=q_cm.bitcast(I32)[:, 0:256], in0=q_sc.bitcast(I32)[:, 0:256], scalar1=0x7F800000,
                        scalar2=None, op0=ALU.bitwise_and)
        e.tensor_scalar(out=q_cm, in0=q_cm, scalar1=1572864.0, scalar2=None, op0=ALU.mult)
        e.tensor_tensor(out=q_t1, in0=q_sc, in1=q_cm, op=ALU.add)
        e.tensor_tensor(out=q_bm, in0=q_t1, in1=q_cm, op=ALU.subtract)
        e.reciprocal(q_rec, q_bm)
        e.tensor_scalar(out=q_rec, in0=q_rec, scalar1=gs_ap, scalar2=1e30, op0=ALU.mult, op1=ALU.min)
        ctx.wait(e, "qS", 2 * i + 1)
        recb = q_rec[:, 0:128].rearrange("p (nb o) -> p nb o", o=1).broadcast_to([128, NBQ, 16])
        ctx.inc(e.tensor_tensor(out=s_a.rearrange("p (nb b) -> p nb b", b=16),
                                in0=s_abs.rearrange("p (nb b) -> p nb b", b=16),
                                in1=recb, op=ALU.mult), "qV")
        e.tensor_scalar(out=s_q1, in0=s_a, scalar1=C_HALF, scalar2=C_HALF, op0=ALU.add, op1=ALU.subtract)
        e.tensor_scalar(out=mskq[:], in0=s_a, scalar1=2.25, scalar2=None, op0=ALU.is_ge)
        ctx.wait(e, "qS", 2 * i + 2)
        ctx.wait(e, "qG", i + 1)
        e.scalar_tensor_tensor(out=s_abs, in0=s_q2, scalar=4.0, in1=s_g2,
                               op0=ALU.min, op1=ALU.add)
        e.copy_predicated(out=s_q1, mask=mskq[:], data=s_abs)
        scb = q_bm[:, 0:128].rearrange("p (nb o) -> p nb o", o=1).broadcast_to([128, NBQ, 16])
        e.scalar_tensor_tensor(out=s_a.bitcast(BF16)[:, 0:2048].rearrange("p (nb b) -> p nb b", b=16),
                               in0=s_q1.rearrange("p (nb b) -> p nb b", b=16),
                               scalar=6.0, in1=scb, op0=ALU.min, op1=ALU.mult)
        ctx.inc(e.tensor_tensor(out=qbf, in0=s_a.bitcast(BF16)[:, 0:2048],
                                in1=sgn_b, op=ALU.mult), "qV")

    def q_s(e, i, x_ap):
        ctx.wait(e, "qV", 2 * i)
        e.activation(out=s_abs, in_=x_ap, func=AF.Abs)
        ctx.inc(e.activation(out=sgn_b, in_=x_ap, func=AF.Sign), "qS")
        ctx.wait(e, "qV", 2 * i + 1)
        e.activation(out=s_q2, in_=s_a, func=AF.Identity, bias=consts[:, 5:6])
        ctx.inc(e.activation(out=s_q2, in_=s_q2, func=AF.Identity, bias=consts[:, 6:7]), "qS")

    def q_g(e, i):
        ctx.wait(e, "qV", 2 * i + 1)
        ctx.inc(e.tensor_scalar(out=s_g2, in0=s_a, scalar1=5.0, scalar2=2.0,
                                op0=ALU.is_ge, op1=ALU.mult), "qG")

    strip_g = mgf32(0, 4096)
    strip_u = mgf32(16, 4096)
    strip_d = mgf32(32, 4096)
    stag_gu = mgbf(48, 4096).rearrange("p (k n) -> p k n", k=8)
    stag_d = mgbf(48, 4096).rearrange("p (k n) -> p k n", k=4)
    with nc.Block() as b:
        W8 = "p8w"
        V8 = "p8v"
        G8 = "p8g"
        T8 = "p8t"
        S8 = "p8s"
        D8 = "p8d"
        QS = [(strip_g, 0, 11, 14), (strip_g, 2048, 11, 14),
              (strip_u, 0, 12, 15), (strip_u, 2048, 12, 15),
              (strip_d, 0, 13, 16), (strip_d, 2048, 13, 16)]
        qbase = ctx.cnt["qV"] // 2

        @b.sync
        def _(e):
            ctx.inc(e.dma_start(out=strip_g.rearrange("p (j d) -> p j d", d=D),
                                in_=d_gate_s.rearrange("(j p) d -> p j d", p=128)), W8, 16)
            ctx.inc(e.dma_start(out=strip_u.rearrange("p (j d) -> p j d", d=D),
                                in_=d_up_s.rearrange("(j p) d -> p j d", p=128)), W8, 16)
            ctx.inc(e.dma_start(out=strip_d.rearrange("p (o i) -> p o i", i=512),
                                in_=d_down_s.rearrange("(o p) i -> p o i", p=128)), W8, 16)
            for w in range(3):
                ctx.wait(e, S8, 4 * (w + 1))
                ctx.inc(e.dma_start(out=wqg_in.ap()[w:w + 1, :, :].squeeze(),
                                    in_=mgbf(48, 4096)), D8, 16)

        @b.vector
        def _(e):
            for w, (st, off) in enumerate([(strip_g, 16), (strip_u, 32), (strip_d, 48)]):
                ctx.wait(e, W8, off)
                ctx.inc(e.tensor_reduce(out=gsv[:, w:w + 1], in_=st, axis=AX, op=ALU.max,
                                        apply_absolute_value=True), V8)
            ctx.wait(e, "p8gz", 1)
            ctx.wait(e, "p8g2", 1)
            ctx.inc(e.tensor_reduce(out=mxt[0:16, 128:129], in_=mxt[0:16, 0:128],
                                    axis=AX, op=ALU.max), "p8g3")
            ctx.wait(e, G8, 16)
            e.drain()
            e.reciprocal(gsv[:, 8:11], gsv[:, 4:7])
            e.reciprocal(gsv[:, 20:22], consts[:, 1:3])
            e.drain()
            e.tensor_scalar(out=gsv[:, 11:14], in0=gsv[:, 8:11], scalar1=2688.0, scalar2=None, op0=ALU.mult)
            e.tensor_scalar(out=gsv[:, 14:17], in0=gsv[:, 8:11], scalar1=448.0, scalar2=None, op0=ALU.mult)
            e.tensor_scalar(out=gsv[:, 26:28], in0=consts[:, 1:3], scalar1=1.0 / 6.0, scalar2=None, op0=ALU.mult)
            e.tensor_scalar(out=gsv[:, 17:18], in0=gsv[:, 4:5], scalar1=gsv[:, 20:21], scalar2=1.0 / 2688.0,
                            op0=ALU.mult, op1=ALU.mult)
            e.tensor_scalar(out=gsv[:, 18:19], in0=gsv[:, 5:6], scalar1=gsv[:, 20:21], scalar2=1.0 / 2688.0,
                            op0=ALU.mult, op1=ALU.mult)
            e.tensor_scalar(out=gsv[:, 19:20], in0=gsv[:, 6:7], scalar1=gsv[:, 21:22], scalar2=0.5 / 2688.0,
                            op0=ALU.mult, op1=ALU.mult)
            ctx.inc(e.drain(), V8)
            for i6 in range(6):
                if i6:
                    ctx.wait(e, T8, 32 * i6)
                st, boff, gcol, g6col = QS[i6]
                q_v(e, qbase + i6, st[:, boff:boff + 2048], gsv[:, gcol:gcol + 1], gsv[:, g6col:g6col + 1])

        @b.scalar
        def _(e):
            ctx.wait(e, V8, 4)
            for i6 in range(6):
                st, boff, _, _ = QS[i6]
                q_s(e, qbase + i6, st[:, boff:boff + 2048])
                for j2 in range(2):
                    ctx.wait(e, T8, 16 * (2 * i6 + j2 + 1))
                    w = i6 // 2
                    if w < 2:
                        j = 2 * (i6 % 2) + j2
                        dst = stag_gu[:, :, 128 * j:128 * (j + 1)]
                        srcv = ps_tr[:].rearrange("p (k n) -> p k n", n=128)
                    else:
                        o = 4 * (i6 % 2) + 2 * j2
                        dst = stag_d[:, :, 128 * o:128 * (o + 2)].rearrange("p k (o2 n) -> p k o2 n", o2=2)
                        srcv = ps_tr[:].rearrange("p (o2 i4 n) -> p i4 o2 n", o2=2, n=128)
                    ctx.inc(e.activation(out=dst, in_=srcv, func=AF.Identity), S8)

        @b.tensor
        def _(e):
            for i6 in range(6):
                ctx.wait(e, "qV", 2 * (qbase + i6 + 1))
                w = i6 // 2
                for j2 in range(2):
                    if not (i6 == 0 and j2 == 0):
                        ctx.wait(e, S8, 2 * i6 + j2)
                    if w < 2:
                        for k in range(8):
                            ctx.inc(e.transpose(ps_tr[:, 128 * k:128 * (k + 1)],
                                                qbf[:, 1024 * j2 + 128 * k:1024 * j2 + 128 * (k + 1)],
                                                iden[:]), T8, 2)
                    else:
                        for o2 in range(2):
                            for i4 in range(4):
                                ctx.inc(e.transpose(ps_tr[:, 512 * o2 + 128 * i4:512 * o2 + 128 * (i4 + 1)],
                                                    qbf[:, 1024 * j2 + 512 * o2 + 128 * i4:
                                                        1024 * j2 + 512 * o2 + 128 * (i4 + 1)],
                                                    iden[:]), T8, 2)

        @b.gpsimd
        def _(e):
            ctx.inc(e.memset(gsv[:, 3:16], 0.0), "p8gz", 1)
            ctx.wait(e, V8, 3)
            ctx.inc(e.dma_start(out=cc_mx_in.ap(), in_=gsv[:, 0:16]), "p8mxd", 16)
            ctx.wait(e, "p8mxd", 16)
            ctx.inc(e.collective_compute(
                "AllReduce", ALU.max,
                replica_groups=[[0, 1, 2, 3, 4, 5, 6, 7]],
                ins=[cc_mx_in.ap().opt()],
                outs=[cc_mx_out.ap().opt()]), "p8mxc", 1)
            ctx.wait(e, "p8mxc", 1)
            with nc.allow_non_contiguous_dma(reason="one-time 8KB gmax transpose"):
                ctx.inc(e.dma_start(out=mxt[0:16, 0:128], in_=cc_mx_out.ap().transpose([1, 0])), "p8mxd", 16)
            ctx.wait(e, "p8mxd", 32)
            ctx.inc(e.drain(), "p8g2", 1)
            ctx.wait(e, "p8g3", 1)
            ctx.inc(e.dma_start(out=mx_sc.ap(), in_=mxt[0:16, 128:129]), "p8mxd", 16)
            ctx.wait(e, "p8mxd", 48)
            ctx.inc(e.dma_start(out=gsv[:, 4:20],
                                in_=mx_sc.ap().rearrange("(o n) -> o n", o=1).broadcast_to([128, 16])),
                    G8, 16)
            for i6 in range(6):
                q_g(e, qbase + i6)
            ctx.wait(e, D8, 48)
            ctx.inc(e.collective_compute(
                "AllGather", ALU.bypass,
                replica_groups=[[0, 1, 2, 3, 4, 5, 6, 7]],
                ins=[wqg_in.ap().opt()],
                outs=[wqg_out.ap().opt()]), "p8wgc", 1)
            ctx.wait(e, "p8wgc", 1)

    if _KMAX < 9:
        emit_store(ctx, nc, dout, resid, ST)
        stack.close()
        return nc

    silu_buf = silu_buf2

    def emit_l1ffn(tag):
        base = snap(*LNSEMS)
        SAP = tag + "sap"
        TXQ = tag + "txq"
        SEV = tag + "sev"
        WGU = tag + "wgu"
        WDQ = tag + "wdq"
        TGU = tag + "tgu"
        SSG = tag + "ssg"
        VHD = tag + "vhd"
        THQ = tag + "thq"
        TDN = tag + "tdn"
        ACC = tag + "acc"
        qb0 = ctx.cnt["qV"] // 2

        def qidx(s, j):
            return qb0 + 10 * s + j

        with nc.Block() as b:
            @b.sync
            def _(e):
                for s in range(4):
                    for ei in range(8):
                        E = 8 * s + ei
                        pb = E % 2
                        if E >= 2:
                            ctx.wait(e, TGU, 8 * (E - 1))
                        ctx.inc(e.dma_start(out=wgu[pb][:, 0, :, :],
                                            in_=wqg_out.ap()[ei:ei + 1, 0:1, :, :].squeeze()), WGU, 16)
                        ctx.inc(e.dma_start(out=wgu[pb][:, 1, :, :],
                                            in_=wqg_out.ap()[ei:ei + 1, 1:2, :, :].squeeze()), WGU, 16)
                        if E >= 2:
                            ctx.wait(e, TDN, 8 * (E - 1))
                        ctx.inc(e.dma_start(out=wdq[pb][:, :, :],
                                            in_=wqg_out.ap()[ei:ei + 1, 2:3, :, :].squeeze()), WDQ, 16)

            @b.scalar
            def _(e):
                ln_prepass_s(e, base)
                ctx.wait(e, "lnp_v3", base["lnp_v3"] + 1)
                for s in range(4):
                    for h in range(2):
                        q = qidx(s, h)
                        t0 = 4 * s + 2 * h
                        ctx.wait(e, "qV", 2 * q)
                        for tt2 in range(2):
                            ctx.inc(e.activation(out=hid_buf[:, 1024 * tt2:1024 * (tt2 + 1)],
                                                 in_=resid[:, t0 + tt2, :], func=AF.Identity,
                                                 scale=lnR[:, t0 + tt2:t0 + tt2 + 1],
                                                 bias=lnB[:, t0 + tt2:t0 + tt2 + 1]), SAP)
                        q_s(e, q, hid_buf)
                        for tt2 in range(2):
                            g4 = 2 * h + tt2
                            ctx.wait(e, TXQ, 32 * s + 8 * (g4 + 1))
                            ctx.inc(e.activation(out=xq_fm[:, :, 128 * g4:128 * g4 + 128],
                                                 in_=ps_tr[:].rearrange("p (k n) -> p k n", n=128),
                                                 func=AF.Identity), SEV)
                    for ei in range(8):
                        E = 8 * s + ei
                        q = qidx(s, 2 + ei)
                        for tt in range(4):
                            g = 4 * E + tt
                            ctx.wait(e, TGU, 2 * g + 1)
                            if g > 0:
                                ctx.wait(e, VHD, g)
                            ctx.inc(e.activation(out=silu_buf, in_=psA[0][:, 0:512], func=AF.Silu,
                                                 scale=gsv[:, 17:18]), SSG)
                        ctx.wait(e, VHD, 4 * (E + 1))
                        q_s(e, q, hid_buf)
                        for tt in range(4):
                            ctx.wait(e, THQ, 16 * E + 4 * (tt + 1))
                            ctx.inc(e.activation(out=hq_fm[:, :, 128 * tt:128 * (tt + 1)],
                                                 in_=ps_tr[:, 0:512].rearrange("p (k n) -> p k n", n=128),
                                                 func=AF.Identity), SEV)
                        if ei == 0:
                            for tt in range(4):
                                d = 4 * E + tt
                                ctx.wait(e, TDN, 2 * d + 1)
                                ctx.inc(e.activation(out=down_acc[:, 1024 * tt:1024 * tt + 512],
                                                     in_=psA[2][:, 0:512], func=AF.Identity), ACC)
                                ctx.wait(e, TDN, 2 * d + 2)
                                ctx.inc(e.activation(out=down_acc[:, 1024 * tt + 512:1024 * (tt + 1)],
                                                     in_=psA[3][:, 0:512], func=AF.Identity), ACC)

            @b.vector
            def _(e):
                ln_prepass_v(e, base)
                for s in range(4):
                    for h in range(2):
                        q = qidx(s, h)
                        ctx.wait(e, SAP, 4 * s + 2 * h + 2)
                        if h == 1:
                            ctx.wait(e, TXQ, 32 * s + 16)
                        elif s > 0:
                            ctx.wait(e, THQ, 16 * 8 * s)
                        q_v(e, q, hid_buf, consts[:, 1:2], gsv[:, 26:27])
                    for ei in range(8):
                        E = 8 * s + ei
                        q = qidx(s, 2 + ei)
                        ctx.wait(e, "qS", 2 * (q - 1) + 1)
                        if ei == 0:
                            ctx.wait(e, TXQ, 32 * (s + 1))
                        else:
                            ctx.wait(e, THQ, 16 * E)
                        for tt in range(4):
                            g = 4 * E + tt
                            ctx.wait(e, TGU, 2 * g + 2)
                            ctx.wait(e, SSG, g + 1)
                            ctx.inc(e.scalar_tensor_tensor(out=hid_buf[:, 512 * tt:512 * (tt + 1)],
                                                           in0=psA[1][:, 0:512], scalar=gsv[:, 18:19],
                                                           in1=silu_buf, op0=ALU.mult, op1=ALU.mult), VHD)
                        q_v(e, q, hid_buf, consts[:, 2:3], gsv[:, 27:28])
                        if ei > 0:
                            for tt in range(4):
                                d = 4 * E + tt
                                if ei < 7:
                                    ctx.wait(e, TDN, 2 * d + 1)
                                    ctx.inc(e.tensor_tensor(out=down_acc[:, 1024 * tt:1024 * tt + 512],
                                                            in0=psA[2][:, 0:512],
                                                            in1=down_acc[:, 1024 * tt:1024 * tt + 512],
                                                            op=ALU.add), ACC)
                                    ctx.wait(e, TDN, 2 * d + 2)
                                    ctx.inc(e.tensor_tensor(out=down_acc[:, 1024 * tt + 512:1024 * (tt + 1)],
                                                            in0=psA[3][:, 0:512],
                                                            in1=down_acc[:, 1024 * tt + 512:1024 * (tt + 1)],
                                                            op=ALU.add), ACC)
                                else:
                                    t = 4 * s + tt
                                    for nh in range(2):
                                        ctx.wait(e, TDN, 2 * d + 1 + nh)
                                        asl = down_acc[:, 1024 * tt + 512 * nh:1024 * tt + 512 * (nh + 1)]
                                        e.tensor_tensor(out=asl, in0=psA[2 + nh][:, 0:512], in1=asl, op=ALU.add)
                                        ctx.inc(e.scalar_tensor_tensor(
                                            out=resid[:, t, 512 * nh:512 * (nh + 1)],
                                            in0=asl, scalar=gsv[:, 19:20],
                                            in1=resid[:, t, 512 * nh:512 * (nh + 1)],
                                            op0=ALU.mult, op1=ALU.add), ACC)

            @b.gpsimd
            def _(e):
                for s in range(4):
                    for j in range(10):
                        q_g(e, qidx(s, j))

            @b.tensor
            def _(e):
                nev = 0
                for s in range(4):
                    for g4 in range(4):
                        h, tt2 = g4 // 2, g4 % 2
                        q = qidx(s, h)
                        ctx.wait(e, "qV", 2 * (q + 1))
                        if nev:
                            ctx.wait(e, SEV, nev)
                        nev += 1
                        for k in range(8):
                            ctx.inc(e.transpose(ps_tr[:, 128 * k:128 * (k + 1)],
                                                qbf[:, 1024 * tt2 + 128 * k:1024 * tt2 + 128 * (k + 1)],
                                                iden[:]), TXQ)
                    for ei in range(8):
                        E = 8 * s + ei
                        q = qidx(s, 2 + ei)
                        for tt in range(4):
                            g = 4 * E + tt
                            ctx.wait(e, SEV, 36 * s + 4)
                            ctx.wait(e, WGU, 32 * (E + 1))
                            if g > 0:
                                ctx.wait(e, SSG, g)
                                ctx.wait(e, VHD, g)
                            for k in range(8):
                                i = e.matmul(psA[0][:, 0:512], xq_fm[:, k, 128 * tt:128 * (tt + 1)],
                                             wgu[E % 2][:, 0, k, :], start=(k == 0), stop=(k == 7))
                            ctx.inc(i, TGU)
                            for k in range(8):
                                i = e.matmul(psA[1][:, 0:512], xq_fm[:, k, 128 * tt:128 * (tt + 1)],
                                             wgu[E % 2][:, 1, k, :], start=(k == 0), stop=(k == 7))
                            ctx.inc(i, TGU)
                        ctx.wait(e, "qV", 2 * (q + 1))
                        for tt in range(4):
                            if nev:
                                ctx.wait(e, SEV, nev)
                            nev += 1
                            for i4 in range(4):
                                ctx.inc(e.transpose(ps_tr[:, 128 * i4:128 * (i4 + 1)],
                                                    qbf[:, 512 * tt + 128 * i4:512 * tt + 128 * (i4 + 1)],
                                                    iden[:]), THQ)
                        ctx.wait(e, WDQ, 16 * (E + 1))
                        for tt in range(4):
                            d = 4 * E + tt
                            ctx.wait(e, SEV, 36 * s + 4 + 4 * ei + tt + 1)
                            if d > 0:
                                ctx.wait(e, ACC, 2 * d)
                            for nh in range(2):
                                for i4 in range(4):
                                    i = e.matmul(psA[2 + nh][:, 0:512],
                                                 hq_fm[:, i4, 128 * tt:128 * (tt + 1)],
                                                 wdq[E % 2][:, i4, 512 * nh:512 * (nh + 1)],
                                                 start=(i4 == 0), stop=(i4 == 3))
                                ctx.inc(i, TDN)

    emit_l1ffn("f2")

    if _KMAX < 10:
        emit_store(ctx, nc, dout, resid, ST)
        stack.close()
        return nc

    d_wqkv1 = nc.dram_tensor("L1_attn_w_qkv", [D, 3 * D], F32, kind="ExternalInput").ap()
    d_proj1 = nc.dram_tensor("L1_attn_proj", [128, 64], F32, kind="ExternalInput").ap()
    d_wout1 = nc.dram_tensor("L1_attn_w_out", [D, D], F32, kind="ExternalInput").ap()
    d_bout1 = nc.dram_tensor("bout1_b", [128, D], F32, kind="ExternalInput").ap()
    d_mm_sh = nc.dram_tensor("mm_sh", [128, 128], F32, kind="ExternalInput").ap()
    d_mm_t0 = nc.dram_tensor("mm_t0", [128, 128], F32, kind="ExternalInput").ap()
    d_me = nc.dram_tensor("me3", [6, 256], F32, kind="ExternalInput").ap()
    d_wsl = nc.dram_tensor("wsel_l", [128, 8], BF16, kind="ExternalInput").ap()
    d_wsr = nc.dram_tensor("wsel_r", [128, 8], BF16, kind="ExternalInput").ap()
    cc_hl_in = nc.dram_tensor("cc_hl_in", [4656], BF16)
    cc_hl_out = nc.dram_tensor("cc_hl_out", [8, 4656], BF16)

    wq_hg = mgbf(0, 12288).rearrange("p (k n) -> p k n", k=8)
    S_sb = mgbf(0, 4096).rearrange("p (i n) -> p i n", i=32)
    SE_sb = mgbf(8, 4096).rearrange("p (i n) -> p i n", i=32)
    wout1 = mgbf(0, 8192).rearrange("p (h n) -> p h n", h=8)
    x_fm1 = mgbf(24, 16384).rearrange("p (k n) -> p k n", k=8)
    afm1 = mgbf(56, 16384).rearrange("p (h n) -> p h n", h=8)
    v_s4 = mgbf(88, 9792).rearrange("p (h t n) -> p h t n", h=4, t=17)
    qh_t4 = mgbf(107.25, 512).rearrange("p (h n) -> p h n", h=4)
    qh_fl = mgbf(107.25, 512)
    attn_tok2 = mgbf(107.25, 512)
    kphi1 = mgbf(108.5, 4128).rearrange("p (r n) -> p r n", r=2)
    qphi1 = mgbf(116.75, 4128).rearrange("p (r n) -> p r n", r=2)
    elu_r = mgf32(125, 512)
    elu_m = mgf32(127, 512)
    hl_v1 = mgbf(125, 1032)
    hl_k2 = mgbf(127.25, 96)
    k_stg = mgbf(129, 512).rearrange("p (s r n) -> p s r n", s=2, r=2)
    q_stg = mgbf(130, 512).rearrange("p (s r n) -> p s r n", s=2, r=2)
    v_aln = mgbf(131, 1024).rearrange("p (s h n) -> p s h n", s=2, h=4)
    v_aln_fl = mgbf(131, 1024)
    hs_tmp = mgbf(133, 516)
    proj_s = lnM[:].bitcast(BF16)[:, 0:64]
    wsel_l = lnM[:].bitcast(BF16)[:, 64:72]
    wsel_r = lnM[:].bitcast(BF16)[:, 72:80]
    mm_sh = lnV[:]
    mm_t0 = lnRn[:]
    me_std = lnR[0:6, 0:128]
    me_t15 = lnB[0:6, 0:128]
    zsc = lnS[:][:, 32:48]
    bout1 = halfbias

    AS = ("awq", "atqk", "asex", "asev", "avel", "agrl", "agva", "agdm",
          "agcc", "avhl", "atsm", "avsm", "atat", "avz", "asat", "attr", "aghd")
    for _n in AS:
        ctx.sem(_n)
    (WQ3, TQK, SEX, SEV, VEL, GRL, GVA, GDM,
     GCC, VHL, TSM, VSM, TAT, VZ, SAT, TTR2, GHD) = AS
    KF, KL, VF, VL = 0, 768, 1536, 3084

    def emit_l1attn_hg(hg):
        bs = snap(*AS)

        with nc.Block() as b:
            @b.gpsimd
            def _(e):
                if hg == 0:
                    ctx.inc(e.dma_start(out=proj_s, in_=d_proj1), WQ3, 16)
                    ctx.inc(e.dma_start(out=wsel_l, in_=d_wsl), WQ3, 16)
                    ctx.inc(e.dma_start(out=wsel_r, in_=d_wsr), WQ3, 16)
                    ctx.inc(e.dma_start(out=mm_sh, in_=d_mm_sh), WQ3, 16)
                    ctx.inc(e.dma_start(out=mm_t0, in_=d_mm_t0), WQ3, 16)
                    ctx.inc(e.dma_start(out=me_std, in_=d_me[:, 0:128]), WQ3, 16)
                    ctx.inc(e.dma_start(out=me_t15, in_=d_me[:, 128:256]), WQ3, 16)
                for w3 in range(3):
                    ctx.inc(e.dma_start(
                        out=wq_hg[:, :, 512 * w3:512 * (w3 + 1)],
                        in_=d_wqkv1[:, 1024 * w3 + 512 * hg:1024 * w3 + 512 * (hg + 1)]
                        .rearrange("(k p) n -> p k n", p=128)), WQ3, 16)
                for t in range(NT):
                    ctx.wait(e, SEV, bs[SEV] + 3 * t + 3)
                    ctx.inc(e.dma_start(out=v_s4[3:128, :, t, 0:128],
                                        in_=v_aln[0:125, t % 2, :, :]), GVA, 16)
                    ctx.inc(e.dma_start(out=v_s4[0:3, :, t + 1, 0:128],
                                        in_=v_aln[125:128, t % 2, :, :]), GVA, 16)
                    ctx.wait(e, VEL, bs[VEL] + 6 * t + 3)
                    ctx.inc(e.dma_start(out=qphi1[64:128, :, 128 * t:128 * t + 128],
                                        in_=q_stg[0:64, t % 2, :, :]), GRL, 16)
                    ctx.wait(e, VEL, bs[VEL] + 6 * t + 6)
                    ctx.inc(e.dma_start(out=kphi1[64:128, :, 128 * t + 3:128 * t + 131],
                                        in_=k_stg[0:64, t % 2, :, :]), GRL, 16)
                ctx.wait(e, GRL, bs[GRL] + 16 * 32)
                ctx.wait(e, GVA, bs[GVA] + 16 * 32)
                cin = cc_hl_in.ap()
                ctx.inc(e.dma_start(out=cin[KF:KF + 768].rearrange("(p r n) -> p r n", p=128, r=2),
                                    in_=kphi1[:, :, 3:6]), GDM, 16)
                ctx.inc(e.dma_start(out=cin[KL:KL + 768].rearrange("(p r n) -> p r n", p=128, r=2),
                                    in_=kphi1[:, :, 2048:2051]), GDM, 16)
                ctx.inc(e.dma_start(out=cin[VF:VF + 1548].rearrange("(p h n) -> p h n", p=3, h=4),
                                    in_=v_s4[3:6, :, 0, 0:129]), GDM, 16)
                ctx.inc(e.dma_start(out=cin[VL:VL + 1548].rearrange("(p h n) -> p h n", p=3, h=4),
                                    in_=v_s4[0:3, :, 16, 0:129]), GDM, 16)
                ctx.wait(e, GDM, bs[GDM] + 64)
                ctx.inc(e.collective_compute(
                    "AllGather", ALU.bypass,
                    replica_groups=[[0, 1, 2, 3, 4, 5, 6, 7]],
                    ins=[cc_hl_in.ap().opt()],
                    outs=[cc_hl_out.ap().opt()]), GCC, 1)
                ctx.wait(e, GCC, bs[GCC] + 1)
                flat = cc_hl_out.ap()
                ctx.inc(e.dma_start(out=hl_k2[:, 0:48].rearrange("p (r n) -> p r n", r=8),
                                    in_=flat[:, KL:KL + 768].rearrange("r (p n) -> p r n", p=128)),
                        GDM, 16)
                ctx.inc(e.dma_start(out=hl_k2[:, 48:96].rearrange("p (r n) -> p r n", r=8),
                                    in_=flat[:, KF:KF + 768].rearrange("r (p n) -> p r n", p=128)),
                        GDM, 16)
                for si in range(2):
                    voff = VL if si == 0 else VF
                    for hl in range(4):
                        vidx = 4 * si + hl
                        if vidx > 0:
                            ctx.wait(e, VHL, bs[VHL] + 2 + vidx)
                        ctx.inc(e.dma_start(
                            out=hl_v1[0:3, :].rearrange("p (r n) -> p r n", r=8),
                            in_=flat[:, voff:voff + 1548]
                            .rearrange("r (p h n) -> p h r n", p=3, h=4)[:, hl, :, :]),
                            GDM, 16)
                    ctx.wait(e, VHL, bs[VHL] + 2 + 4 * (si + 1))
                    dstv = (v_s4[0:3, :, 0, 0:129] if si == 0
                            else v_s4[3:6, :, 16, 0:129])
                    ctx.inc(e.dma_start(
                        out=dstv, in_=hs_tmp[0:3, :].rearrange("p (h n) -> p h n", h=4)),
                        GHD, 16)

            wql = ctx.cnt[WQ3]

            @b.scalar
            def _(e):
                for t in range(NT):
                    for w in range(2):
                        ctx.wait(e, TQK, bs[TQK] + 5 * t + 2 * w + 1)
                        ctx.inc(e.activation(out=qh_fl, in_=psA[0][:, 0:512],
                                             func=AF.Identity), SEV)
                        ctx.wait(e, VEL, bs[VEL] + 6 * t + 3 * w + 1)
                        ctx.inc(e.activation(out=psA[1][0:64, 0:512], in_=elu_m[0:64, :],
                                             func=AF.Exp), SEX)
                    ctx.wait(e, TQK, bs[TQK] + 5 * t + 5)
                    if t >= 2:
                        ctx.wait(e, GVA, bs[GVA] + 32 * (t - 1))
                    ctx.inc(e.activation(out=v_aln_fl[:, 512 * (t % 2):512 * (t % 2) + 512],
                                         in_=psA[2][:, 0:512], func=AF.Identity), SEV)
                for eo in range(2):
                    for t in range(NT):
                        ctx.wait(e, VZ, bs[VZ] + 48 * eo + 3 * t + 2)
                        bank = psA[3 + t % 2]
                        for prl in range(2):
                            ctx.inc(e.activation(out=attn_tok2[:, 256 * prl:256 * prl + 128],
                                                 in_=bank[:, 160 * prl:160 * prl + 128],
                                                 func=AF.Identity,
                                                 scale=zsc[:, prl:prl + 1]), SAT)

            @b.vector
            def _(e):
                for t in range(NT):
                    for w in range(2):
                        ctx.wait(e, TQK, bs[TQK] + 5 * t + 2 * w + 2)
                        e.tensor_scalar(out=elu_r[0:64, :], in0=psA[1][0:64, 0:512],
                                        scalar1=0.0, scalar2=None, op0=ALU.max)
                        ctx.inc(e.tensor_scalar(out=elu_m[0:64, :], in0=psA[1][0:64, 0:512],
                                                scalar1=0.0, scalar2=None, op0=ALU.min), VEL)
                        ctx.wait(e, SEX, bs[SEX] + 2 * t + w + 1)
                        if t >= 2:
                            ctx.wait(e, GRL, bs[GRL] + 16 * (2 * (t - 2) + 1 + w))
                        for hl in range(4):
                            prl, eo = hl // 2, hl % 2
                            if eo == 0:
                                if w == 0:
                                    dst = qphi1[0:64, prl, 128 * t:128 * t + 128]
                                else:
                                    dst = kphi1[0:64, prl, 128 * t + 3:128 * t + 131]
                            else:
                                stg = q_stg if w == 0 else k_stg
                                dst = stg[0:64, t % 2, prl, :]
                            i = e.tensor_tensor(out=dst, in0=elu_r[0:64, 128 * hl:128 * (hl + 1)],
                                                in1=psA[1][0:64, 128 * hl:128 * (hl + 1)],
                                                op=ALU.add)
                        ctx.inc(i, VEL)
                        ctx.inc(e.drain(), VEL)
                ctx.wait(e, WQ3, wql)
                ctx.wait(e, GDM, bs[GDM] + 96)
                for (cl, wsl, dstk) in ((0, wsel_l, kphi1[:, :, 0:3]),
                                        (48, wsel_r, kphi1[:, :, 2051:2054])):
                    hv = hl_k2[:, cl:cl + 48].rearrange("p (r n) -> p n r", r=8)
                    wb = wsl.rearrange("p (o r) -> p o r", o=1).broadcast_to([128, 6, 8])
                    e.tensor_tensor(out=hv, in0=hv, in1=wb, op=ALU.mult)
                    e.drain()
                    with nc.allow_low_precision(reason="one-hot rank select, exact in bf16"):
                        e.tensor_reduce(out=dstk, in_=hv, axis=AX, op=ALU.add)
                    ctx.inc(e.drain(), VHL)
                for si in range(2):
                    wsl = wsel_l if si == 0 else wsel_r
                    for hl in range(4):
                        vidx = 4 * si + hl
                        ctx.wait(e, GDM, bs[GDM] + 96 + 16 * (vidx + 1))
                        if vidx == 4:
                            ctx.wait(e, GHD, bs[GHD] + 16)
                        hv = hl_v1[0:3, :].rearrange("p (r n) -> p n r", r=8)
                        wb = wsl[0:3, :].rearrange("p (o r) -> p o r", o=1)\
                            .broadcast_to([3, 129, 8])
                        e.tensor_tensor(out=hv, in0=hv, in1=wb, op=ALU.mult)
                        e.drain()
                        dst = hs_tmp[0:3, 129 * hl:129 * (hl + 1)]
                        with nc.allow_low_precision(reason="one-hot rank select, exact in bf16"):
                            e.tensor_reduce(out=dst, in_=hv, axis=AX, op=ALU.add)
                        ctx.inc(e.drain(), VHL)
                for eo in range(2):
                    for t in range(NT):
                        for prl in range(2):
                            gi = 2 * t + prl
                            ctx.wait(e, TSM, bs[TSM] + 64 * eo + 2 * gi + 2)
                            pm = psA[gi % 2]
                            mmask = mm_t0 if t == 0 else mm_sh
                            e.tensor_tensor(out=S_sb[:, gi, :], in0=pm[:, 0:128],
                                            in1=mmask, op=ALU.mult)
                            emask = me_t15 if t == NT - 1 else me_std
                            ctx.inc(e.tensor_tensor(out=SE_sb[0:6, gi, :], in0=pm[0:6, 128:256],
                                                    in1=emask, op=ALU.mult), VSM)
                    for t in range(NT):
                        ctx.wait(e, TAT, bs[TAT] + 32 * eo + 2 * (t + 1))
                        if eo or t:
                            ctx.wait(e, SAT, bs[SAT] + 32 * eo + 2 * t)
                        bank = psA[3 + t % 2]
                        e.tensor_scalar(out=zsc[:, 0:1], in0=bank[:, 128:129],
                                        scalar1=1e-8, scalar2=None, op0=ALU.add)
                        e.tensor_scalar(out=zsc[:, 1:2], in0=bank[:, 288:289],
                                        scalar1=1e-8, scalar2=None, op0=ALU.add)
                        e.drain()
                        e.reciprocal(zsc[:, 0:2], zsc[:, 0:2])
                        ctx.inc(e.drain(), VZ, 2)
                        ctx.wait(e, TTR2, bs[TTR2] + 32 * eo + 2 * (t + 1))
                        e.tensor_copy(afm1[:, 4 * hg + eo, 128 * t:128 * (t + 1)],
                                      ps_tr[:, 0:128])
                        ctx.inc(e.tensor_copy(afm1[:, 4 * hg + 2 + eo, 128 * t:128 * (t + 1)],
                                              ps_tr[:, 128:256]), VZ)

            @b.tensor
            def _(e):
                ctx.wait(e, WQ3, wql)
                for t in range(NT):
                    for w in range(2):
                        if t or w:
                            ctx.wait(e, SEV, bs[SEV] + 3 * t + w)
                        for hl in range(4):
                            for k in range(8):
                                i = e.matmul(psA[0][:, 128 * hl:128 * (hl + 1)],
                                             wq_hg[:, k, 512 * w + 128 * hl:512 * w + 128 * (hl + 1)],
                                             x_fm1[:, k, 128 * t:128 * (t + 1)],
                                             start=(k == 0), stop=(k == 7))
                        ctx.inc(i, TQK)
                        ctx.wait(e, SEV, bs[SEV] + 3 * t + w + 1)
                        if t or w:
                            ctx.wait(e, VEL, bs[VEL] + 6 * t + 3 * w - 1)
                        for hl in range(4):
                            i = e.matmul(psA[1][0:64, 128 * hl:128 * (hl + 1)], proj_s,
                                         qh_t4[:, hl, :], start=True, stop=True)
                        ctx.inc(i, TQK)
                    if t > 0:
                        ctx.wait(e, SEV, bs[SEV] + 3 * t)
                    for k in range(8):
                        i = e.matmul(psA[2][:, 0:512], x_fm1[:, k, 128 * t:128 * (t + 1)],
                                     wq_hg[:, k, 1024:1536], start=(k == 0), stop=(k == 7))
                    ctx.inc(i, TQK)
                ctx.wait(e, VHL, bs[VHL] + 10)
                ctx.wait(e, GHD, bs[GHD] + 32)
                for eo in range(2):
                    kb = 64 * eo
                    for t in range(NT):
                        for prl in range(2):
                            gi = 2 * t + prl
                            pm = psA[gi % 2]
                            if 32 * eo + gi >= 2:
                                ctx.wait(e, VSM, bs[VSM] + 32 * eo + gi - 1)
                            qc = qphi1[kb:kb + 64, prl, 128 * t:128 * t + 128]
                            e.matmul(pm[:, 0:128],
                                     kphi1[kb:kb + 64, prl, 128 * t:128 * t + 128],
                                     qc, start=True, stop=True)
                            i = e.matmul(pm[0:6, 128:256],
                                         kphi1[kb:kb + 64, prl, 128 * t + 128:128 * t + 134],
                                         qc, start=True, stop=True)
                            ctx.inc(i, TSM, 2)
                    ctx.wait(e, VSM, bs[VSM] + 32 * (eo + 1))
                    for t in range(NT):
                        if t >= 2:
                            ctx.wait(e, SAT, bs[SAT] + 32 * eo + 2 * (t - 1))
                        elif eo == 1:
                            ctx.wait(e, SAT, bs[SAT] + 2 * (14 + t) + 2)
                        bank = psA[3 + t % 2]
                        for prl in range(2):
                            gi = 2 * t + prl
                            hl = 2 * prl + eo
                            out_ = bank[:, 160 * prl:160 * prl + 129]
                            e.matmul(out_, S_sb[:, gi, :], v_s4[:, hl, t, 0:129],
                                     start=True, stop=False)
                            i = e.matmul(out_, SE_sb[0:6, gi, :], v_s4[0:6, hl, t + 1, 0:129],
                                         start=False, stop=True)
                        ctx.inc(i, TAT, 2)
                        if t >= 1:
                            tt = t - 1
                            ctx.wait(e, SAT, bs[SAT] + 32 * eo + 2 * (tt + 1))
                            if eo or tt:
                                ctx.wait(e, VZ, bs[VZ] + 48 * eo + 3 * tt)
                            e.transpose(ps_tr[:, 0:128], attn_tok2[:, 0:128], iden[:])
                            i = e.transpose(ps_tr[:, 128:256], attn_tok2[:, 256:384], iden[:])
                            ctx.inc(i, TTR2, 2)
                    tt = NT - 1
                    ctx.wait(e, SAT, bs[SAT] + 32 * eo + 2 * (tt + 1))
                    ctx.wait(e, VZ, bs[VZ] + 48 * eo + 3 * tt)
                    e.transpose(ps_tr[:, 0:128], attn_tok2[:, 0:128], iden[:])
                    i = e.transpose(ps_tr[:, 128:256], attn_tok2[:, 256:384], iden[:])
                    ctx.inc(i, TTR2, 2)

    def emit_l1attn():
        base = snap(*LNSEMS)
        SAP = "a3sap"
        TTR = "a3ttr"
        VEV = "a3vev"
        LD3 = "a3ld"

        with nc.Block() as b:
            @b.sync
            def _(e):
                ctx.inc(e.dma_start(out=bout1[:], in_=d_bout1), LD3, 16)

            @b.scalar
            def _(e):
                ln_prepass_s(e, base)
                ctx.wait(e, "lnp_v3", base["lnp_v3"] + 1)
                for t in range(NT):
                    if t > 0:
                        ctx.wait(e, TTR, 8 * t)
                    ctx.inc(e.activation(out=xln_bf[:], in_=resid[:, t, :], func=AF.Identity,
                                         scale=lnR[:, t:t + 1], bias=lnB[:, t:t + 1]), SAP)

            @b.tensor
            def _(e):
                for t in range(NT):
                    ctx.wait(e, SAP, t + 1)
                    if t > 0:
                        ctx.wait(e, VEV, t)
                    for k in range(8):
                        ctx.inc(e.transpose(ps_tr[:, 128 * k:128 * (k + 1)],
                                            xln_bf[:, 128 * k:128 * (k + 1)], iden[:]), TTR)

            @b.vector
            def _(e):
                ln_prepass_v(e, base)
                one1 = consts[:, 4:5].rearrange("p (a o) -> p a o", o=1)
                for h in range(4):
                    e.tensor_tensor(out=v_s4[:, h, :, 128:129],
                                    in0=one1.broadcast_to([128, 17, 1]),
                                    in1=one1.broadcast_to([128, 17, 1]), op=ALU.mult)
                for t in range(NT):
                    ctx.wait(e, TTR, 8 * (t + 1))
                    ctx.inc(e.tensor_copy(x_fm1[:, :, 128 * t:128 * (t + 1)],
                                          ps_tr[:].rearrange("p (k n) -> p k n", n=128)), VEV)

        for hg in range(2):
            emit_l1attn_hg(hg)

        T4 = "a3t4"
        V4 = "a3v4"
        with nc.Block() as b:
            @b.gpsimd
            def _(e):
                ctx.inc(e.dma_start(out=wout1,
                                    in_=d_wout1.rearrange("(h p) n -> p h n", p=128)), "a3ld4", 16)
            ld4 = ctx.cnt["a3ld4"]

            @b.tensor
            def _(e):
                ctx.wait(e, "a3ld4", ld4)
                for t in range(NT):
                    if t >= 2:
                        ctx.wait(e, V4, t - 1)
                    for n in range(2):
                        pm = psA[2 + 2 * (t % 2) + n]
                        for h in range(8):
                            i = e.matmul(pm[:, 0:512], afm1[:, h, 128 * t:128 * (t + 1)],
                                         wout1[:, h, 512 * n:512 * (n + 1)],
                                         start=(h == 0), stop=(h == 7))
                    ctx.inc(i, T4)

            @b.vector
            def _(e):
                for t in range(NT):
                    ctx.wait(e, T4, t + 1)
                    for n in range(2):
                        pm = psA[2 + 2 * (t % 2) + n]
                        e.tensor_tensor(out=e_e[:, 512 * n:512 * (n + 1)], in0=pm[:, 0:512],
                                        in1=bout1[:][:, 512 * n:512 * (n + 1)], op=ALU.add)
                    e.tensor_tensor(out=resid[:, t, 0:512], in0=e_e[:, 0:512],
                                    in1=resid[:, t, 0:512], op=ALU.add)
                    i = e.tensor_tensor(out=resid[:, t, 512:1024], in0=e_e[:, 512:1024],
                                        in1=resid[:, t, 512:1024], op=ALU.add)
                    ctx.inc(i, V4)

    emit_l1attn()

    if _KMAX < 11:
        emit_store(ctx, nc, dout, resid, ST)
        stack.close()
        return nc

    ctx.free(["p1s", "p1v", "p1t",
              "f0ttr", "f0vev", "f0tmm", "f0vep", "f0sap",
              "p3w", "p3t", "p3v",
              "a1sap", "a1vre", "a1sex", "a1ttr", "a1vev", "a1tqk", "a1tpf",
              "a1tkv", "a1vqf", "a1vkv",
              "p5t", "p5v", "p5w", "p5g", "p5gc",
              "a2t", "a2s", "a2v", "a2e",
              "f1ttr", "f1vev", "f1tmm", "f1vep", "f1sap",
              "p8w", "p8s", "p8d", "p8v", "p8gz", "p8g2", "p8g3", "p8g",
              "p8t", "p8mxd", "p8mxc", "p8wgc",
              "f2wgu", "f2wdq", "f2tgu", "f2tdn", "f2sap", "f2txq", "f2sev",
              "f2ssg", "f2vhd", "f2thq", "f2acc"])
    emit_l1ffn("f3")

    with nc.Block() as b:
        @b.vector
        def _(e):
            e.tensor_copy(tsc1[:, 0:1], delta)
            e.tensor_copy(tsc1[:, 1:2], atot)
            e.tensor_copy(tsc1[:, 2:3], nhalfalpha[:])
            e.tensor_copy(tsc1[:, 3:4], lnM[:, 0:1])
            e.tensor_copy(tsc1[:, 4:5], lnQ[:, 0:1])
            e.tensor_copy(tsc1[:, 5:6], lnR[:, 0:1])
            e.tensor_copy(tsc1[:, 6:7], lnB[:, 0:1])
            e.tensor_copy(tsc1[:, 16:16 + D // 2], xln_bf[:, 0:D // 2])
    with nc.Block() as b:
        @b.gpsimd
        def _(e):
            ctx.inc(e.dma_start(out=ddbg[:, 0:D], in_=tsc1), ST, 16)
            ctx.inc(e.dma_start(out=ddbg[:, D:2 * D], in_=ternT[:, 0, :]), ST, 16)
            ctx.inc(e.dma_start(out=ddbg[:, 2 * D:3 * D], in_=terng[:, 0, :]), ST, 16)
            ctx.inc(e.dma_start(out=ddbg[:, 3 * D:4 * D], in_=xfm[:]), ST, 16)

    with nc.Block() as b:
        @b.gpsimd
        def _(e):
            ctx.inc(e.dma_start(out=dout.rearrange("(t p) d -> p t d", p=128), in_=resid[:]), ST, 16)
            e.wait_ge(ctx.sems[ST], ctx.cnt[ST])

    stack.close()
    return nc


def _prep_inputs(inputs):
    x = np.ascontiguousarray(np.asarray(inputs["L0_attn_x"], np.float32).reshape(B * T, D))
    iden = np.eye(128, dtype=ml_dtypes.bfloat16)
    bias0_b = np.ascontiguousarray(np.broadcast_to(np.asarray(inputs["L0_ffn_bias"], np.float32), (128, D)))
    sc = np.zeros((128, 8), np.float32)
    sc[:, 0] = np.asarray(inputs["L0_ffn_alpha"], np.float32)[0]
    sc[:, 1] = np.asarray(inputs["L1_ffn_s_in"], np.float32)[0]
    sc[:, 2] = np.asarray(inputs["L1_ffn_s_in_down"], np.float32)[0]
    sc[:, 3] = EPS_LN
    sc[:, 4] = 1.0
    sc[:, 5] = C_ONE
    sc[:, 6] = -C_ONE
    w0 = np.ascontiguousarray(np.asarray(inputs["L0_ffn_weight"], np.float32))
    wqkv0 = np.ascontiguousarray(np.asarray(inputs["L0_attn_w_qkv"], np.float32))
    wout0 = np.ascontiguousarray(np.asarray(inputs["L0_attn_w_out"], np.float32))
    gate1 = np.ascontiguousarray(np.asarray(inputs["L1_ffn_gate_w"], np.float32))
    up1 = np.ascontiguousarray(np.asarray(inputs["L1_ffn_up_w"], np.float32))
    down1 = np.ascontiguousarray(np.asarray(inputs["L1_ffn_down_w"], np.float32))
    wqkv1 = np.ascontiguousarray(np.asarray(inputs["L1_attn_w_qkv"], np.float32))
    proj1 = np.ascontiguousarray(np.asarray(inputs["L1_attn_proj"], np.float32))
    wout1 = np.ascontiguousarray(np.asarray(inputs["L1_attn_w_out"], np.float32))
    bout1_b = np.ascontiguousarray(
        np.broadcast_to(np.asarray(inputs["L1_attn_b_out"], np.float32), (128, D)))
    _n = np.arange(128)[:, None]
    _m = np.arange(128)[None, :]
    mm_sh = ((_n - _m >= 0) & (_n - _m <= 6)).astype(np.float32)
    _ne = np.arange(6)[:, None]
    me_std = (np.arange(128)[None, :] >= _ne + 122).astype(np.float32)
    in_maps = []
    for c in range(NCORE):
        pos = (TPC * (c % 4) + np.arange(TPC).reshape(NT, 128)).astype(np.float32)
        inv_freq = (np.float32(1.0) /
                    (np.float32(10000.0) ** (np.arange(0, HD0, 2, dtype=np.float32) / np.float32(HD0))))
        freqs = pos[:, :, None] * inv_freq[None, None, :]
        freqs = freqs.astype(np.float32)
        ckv = np.cos(freqs).astype(np.float32).transpose(1, 0, 2).reshape(128, NT * 32)
        skv = np.sin(freqs).astype(np.float32).transpose(1, 0, 2).reshape(128, NT * 32)
        im = {
            "xs": np.ascontiguousarray(x[c * TPC:(c + 1) * TPC]),
            "L0_ffn_weight": w0,
            "L0_attn_w_qkv": wqkv0,
            "L0_attn_w_out": wout0,
            "gate_strip": np.ascontiguousarray(gate1[512 * c:512 * (c + 1)]),
            "up_strip": np.ascontiguousarray(up1[512 * c:512 * (c + 1)]),
            "down_strip": np.ascontiguousarray(down1[:, 512 * c:512 * (c + 1)]),
            "scalars_c": sc,
            "bias0_b": bias0_b,
            "iden": iden,
            "ones128": np.ones((128, 128), np.float32),
            "rope_cos": np.ascontiguousarray(ckv),
            "rope_sin": np.ascontiguousarray(skv),
        }
        mm_t0 = mm_sh.copy()
        if c % 4 == 0:
            mm_t0[0:3, :] = 0.0
        me3 = np.concatenate([me_std, me_std], axis=1)
        if c % 4 == 3:
            me3[3:6, 128:256] = 0.0
        wsl = np.zeros((128, 8), ml_dtypes.bfloat16)
        if c % 4 != 0:
            wsl[:, c - 1] = 1.0
        wsr = np.zeros((128, 8), ml_dtypes.bfloat16)
        if c % 4 != 3:
            wsr[:, c + 1] = 1.0
        im.update({
            "L1_attn_w_qkv": wqkv1,
            "L1_attn_proj": proj1,
            "L1_attn_w_out": wout1,
            "bout1_b": bout1_b,
            "mm_sh": mm_sh,
            "mm_t0": np.ascontiguousarray(mm_t0),
            "me3": np.ascontiguousarray(me3),
            "wsel_l": wsl,
            "wsel_r": wsr,
        })
        in_maps.append(im)
    return in_maps


def kernel(**inputs):
    in_maps = _prep_inputs(inputs)
    nc = build_nc(0)
    res = run_bass_kernel_spmd(nc, in_maps, core_ids=list(range(NCORE)))
    out = np.concatenate([res.results[c]["out"] for c in range(NCORE)], axis=0)
    return out.reshape(B, T, D)


# version 02
# speedup vs baseline: 1.3530x; 1.3530x over previous
import numpy as np
import ml_dtypes
import concourse.bass as bass
import concourse.mybir as mybir
from concourse.bass_utils import run_bass_kernel_spmd
from concourse.alu_op_type import AluOpType as ALU
from bass_rust import ReduceOp
import contextlib, os
from concourse import library_config
_KMAX = int(os.environ.get("KMAX", "99"))

F32 = mybir.dt.float32
BF16 = mybir.dt.bfloat16
U8 = mybir.dt.uint8
I32 = mybir.dt.int32
AF = mybir.ActivationFunctionType
AX = mybir.AxisListType.X

B, T, D = 2, 8192, 1024
NCORE = 8
TPC = T * B // NCORE
NT = TPC // 128
HEADS0, HD0 = 16, 64
HEADS1, HD1, FEAT, WS = 8, 128, 64, 7
EPS_LN = 1e-5

C_HALF = 3.0 * 2.0**21
C_ONE = 3.0 * 2.0**22


class Ctx:

    def __init__(self, nc, stack):
        self.nc = nc
        self.stack = stack
        self.sems = {}
        self.cnt = {}

    def sem(self, name):
        if name not in self.sems:
            self.sems[name] = self.nc.alloc_semaphore(name=name)
            self.cnt[name] = 0
        return self.sems[name]

    def free(self, names):
        hs = [self.sems.pop(n) for n in names if n in self.sems]
        for n in names:
            self.cnt.pop(n, None)
        self.nc.clear_and_free_semaphores(hs)
        self.nc.all_engine_barrier()

    def inc(self, instr, name, n=1):
        instr.then_inc(self.sem(name), n)
        self.cnt[name] += n
        return self.cnt[name]

    def wait(self, eng, name, val):
        eng.wait_ge(self.sem(name), val)


def emit_store(ctx, nc, dout, resid, ST):
    with nc.Block() as b:
        @b.gpsimd
        def _(e):
            ctx.inc(e.dma_start(out=dout.rearrange("(t p) d -> p t d", p=128), in_=resid[:]), ST, 16)
            e.wait_ge(ctx.sems[ST], ctx.cnt[ST])


def build_nc(core_id):
    nc = bass.Bass("TRN2", target_bir_lowering=False, detect_race_conditions=False)
    stack = contextlib.ExitStack()
    ctx = Ctx(nc, stack)

    dx = nc.dram_tensor("xs", [TPC, D], F32, kind="ExternalInput").ap()
    dout = nc.dram_tensor("out", [TPC, D], F32, kind="ExternalOutput").ap()
    ddbg = nc.dram_tensor("dbg", [128, 4 * D], F32, kind="ExternalOutput").ap()
    d_w0 = nc.dram_tensor("L0_ffn_weight", [D, D], F32, kind="ExternalInput").ap()
    d_consts = nc.dram_tensor("scalars_c", [128, 8], F32, kind="ExternalInput").ap()
    d_bias0 = nc.dram_tensor("bias0_b", [128, D], F32, kind="ExternalInput").ap()
    d_iden = nc.dram_tensor("iden", [128, 128], BF16, kind="ExternalInput").ap()
    d_ones128 = nc.dram_tensor("ones128", [128, 128], F32, kind="ExternalInput").ap()
    d_wqkv0 = nc.dram_tensor("L0_attn_w_qkv", [3 * D, D], F32, kind="ExternalInput").ap()
    d_wout0 = nc.dram_tensor("L0_attn_w_out", [D, D], F32, kind="ExternalInput").ap()
    d_ck = nc.dram_tensor("rope_cos", [128, NT * 32], F32, kind="ExternalInput").ap()
    d_sk = nc.dram_tensor("rope_sin", [128, NT * 32], F32, kind="ExternalInput").ap()
    cc_kv_in = nc.dram_tensor("cc_kv_in", [64, 16, 65], F32)
    cc_kv_out = nc.dram_tensor("cc_kv_out", [64, 16, 65], F32)

    def sb(name, shape, dt=F32):
        return stack.enter_context(nc.sbuf_tensor(name, shape, dt))

    resid = sb("resid", [128, NT, D])
    halfbias = sb("halfbias", [128, D])
    iden = sb("iden_s", [128, 128], BF16)
    consts = sb("consts", [128, 8])
    nhalfalpha = sb("nhalfalpha", [128, 1])
    mega = sb("mega", [128, 34688])
    mg = mega[:]
    mgb = mg.bitcast(BF16)

    def mgf32(off_kb, n_f32):
        return mg[:, int(off_kb * 256):int(off_kb * 256) + n_f32]

    def mgbf(off_kb, n_bf):
        return mgb[:, int(off_kb * 512):int(off_kb * 512) + n_bf]

    ternT = mgbf(48, 8192).rearrange("p (c d) -> p c d", d=D)
    wtmp = mgf32(0, 8192).rearrange("p (c d) -> p c d", d=D)
    terng = mgbf(32, 8192).rearrange("p (c d) -> p c d", d=D)
    wqkvT = mgbf(0, 24576).rearrange("p (k n) -> p k n", k=8)
    woutT = mgbf(0, 8192).rearrange("p (k n) -> p k n", k=8)
    ones128 = mgf32(64, 128)
    qfeat = mgbf(64, 16384).rearrange("p (pr n) -> p pr n", pr=8)
    wstage = mgbf(96, 8192).rearrange("p (c d) -> p c d", d=D)
    qrot = mgf32(96, 1024)
    e_r = mgf32(100, 1024)
    e_m = mgf32(104, 1024)
    ropeA = mgf32(108, 256)
    ropeB = mgf32(109, 256)
    ropeC = mgf32(110, 256)
    ropeD = mgf32(111, 256)
    attn_tok = mgbf(96, 1024)
    afm = mgbf(98, 1024)
    xln_bf = mgbf(112, 1024)
    xfm = mgbf(114, 1024)
    e_e = mgf32(116, 1024)
    tsc1t_v = mgf32(120, 1160)
    tsc1 = tsc1t_v[:, 0:1024]
    vaug = mgbf(120, 1280)
    kfeat = mgbf(122.5, 1024)
    qftok = mgbf(125, 1024)
    kv_acc = mg[0:64, int(127 * 256):int(127 * 256) + 1040]
    ropekv = mgf32(131.5, 1024)
    ck = ropekv[:][:, 0:512]
    sk = ropekv[:][:, 512:1024]
    kv_pair = ropekv[:].bitcast(BF16)[:, 0:1280].rearrange("p (pr n) -> p pr n", n=160)
    lnS = sb("lnS", [128, 128])
    lnQ = sb("lnQ", [128, 128])
    lnM = sb("lnM", [128, 128])
    lnV = sb("lnV", [128, 128])
    lnR = sb("lnR", [128, 128])
    lnRn = sb("lnRn", [128, 128])
    lnB = sb("lnB", [128, 128])
    dall = lnS[:][:, 0:16]
    rall = lnS[:][:, 16:32]
    asum = lnV[:][:, 0:16]
    acol = lnM[:][:, 0:1]
    atot = lnM[:][:, 1:2]
    delta = lnM[:][:, 2:3]
    ndelta = lnM[:][:, 3:4]

    def ps(name, shape, dt=F32):
        return stack.enter_context(nc.psum_tensor(name, shape, dt))

    ps_tr = ps("ps_tr", [128, D], BF16)
    psA = [ps(f"psA{i}", [128, 512]) for i in range(6)]
    ps_mm0, ps_mm1 = psA[0], psA[1]
    ps_red = psA[2][:, 0:1]
    ps_qa, ps_qb = psA[0], psA[1]
    ps_kv = [psA[2 + i][0:64, 0:260] for i in range(4)]
    ps_qkv = [psA[2 + i][:, 0:320] for i in range(4)]

    LD = "ld"
    ST = "st"

    with nc.Block() as b:
        @b.sync
        def _(e):
            ctx.inc(e.dma_start(out=resid[:], in_=dx.rearrange("(t p) d -> p t d", p=128)), LD, 16)
            ctx.inc(e.dma_start(out=halfbias[:], in_=d_bias0), LD, 16)
            ctx.inc(e.dma_start(out=iden[:], in_=d_iden), LD, 16)
            ctx.inc(e.dma_start(out=ones128, in_=d_ones128), LD, 16)
            ctx.inc(e.dma_start(out=consts[:], in_=d_consts), LD, 16)
            ctx.inc(e.dma_start(out=wtmp, in_=d_w0.rearrange("(c p) d -> p c d", p=128)), LD, 16)

    with nc.Block() as b:
        @b.gpsimd
        def _(e):
            e.memset(lnS[:], 0.0)
            e.memset(lnQ[:], 0.0)
            e.memset(kv_acc[:], 0.0)
            e.memset(mega[:, 27648:28928], 0.0)

    with nc.Block() as b:
        SQ = "p1s"
        VQ = "p1v"
        GQ = "p1g"
        TQ = "p1t"

        @b.scalar
        def _(e):
            ctx.wait(e, LD, ctx.cnt[LD])
            for c in range(8):
                for hh in range(2):
                    i = ctx.inc(e.activation(out=xln_bf[:].bitcast(F32)[:, 0:512],
                                             in_=wtmp[:, c, 512 * hh:512 * (hh + 1)], func=AF.Abs,
                                             accum_out=asum[:, 2 * c + hh:2 * c + hh + 1]), SQ)

        @b.vector
        def _(e):
            ctx.wait(e, SQ, 16)
            ctx.inc(e.tensor_reduce(out=acol, in_=asum, axis=AX, op=ALU.add), VQ)
            ctx.wait(e, TQ, 1)
            ctx.inc(e.tensor_copy(atot, ps_red), VQ)
            e.drain()
            e.tensor_scalar(out=delta, in0=atot, scalar1=0.7 / (D * D), scalar2=None, op0=ALU.mult)
            e.tensor_scalar(out=ndelta, in0=atot, scalar1=-0.7 / (D * D), scalar2=None, op0=ALU.mult)
            e.tensor_scalar(out=nhalfalpha[:], in0=consts[:, 0:1], scalar1=-0.5, scalar2=None, op0=ALU.mult)
            e.tensor_scalar(out=halfbias[:], in0=halfbias[:], scalar1=0.5, scalar2=None, op0=ALU.mult)
            e.drain()
            for c in range(8):
                e.tensor_scalar(out=tsc1, in0=wtmp[:, c, :], scalar1=delta, scalar2=None, op0=ALU.is_gt)
                ctx.inc(e.scalar_tensor_tensor(out=terng[:, c, :], in0=wtmp[:, c, :], scalar=ndelta,
                                               in1=tsc1, op0=ALU.is_lt, op1=ALU.subtract), VQ)
            for k in range(8):
                ctx.wait(e, TQ, 1 + 8 * (k + 1))
                ctx.inc(e.tensor_copy(ternT[:, k, :], ps_tr[:]), VQ)

        @b.gpsimd
        def _(e):
            pass

        @b.tensor
        def _(e):
            ctx.wait(e, VQ, 1)
            ctx.inc(e.matmul(ps_red, ones128, acol, start=True, stop=True), TQ)
            ctx.wait(e, VQ, 10)
            for k in range(8):
                if k > 0:
                    ctx.wait(e, VQ, 10 + k)
                for o in range(8):
                    ctx.inc(e.transpose(ps_tr[:, 128 * o:128 * (o + 1)],
                                        terng[:, o, 128 * k:128 * (k + 1)], iden[:]), TQ)

    def ln_prepass_v(e, base):
        for t in range(NT):
            e.tensor_reduce(out=lnS[:, t:t + 1], in_=resid[:, t, :], axis=AX, op=ALU.add)
        ctx.wait(e, "lnp_s", base["lnp_s"] + 16)
        e.drain()
        e.tensor_scalar(out=lnM[:], in0=lnS[:], scalar1=1.0 / D, scalar2=None, op0=ALU.mult)
        e.tensor_tensor(out=lnV[:], in0=lnM[:], in1=lnM[:], op=ALU.mult)
        e.scalar_tensor_tensor(out=lnV[:], in0=lnQ[:], scalar=1.0 / D, in1=lnV[:],
                               op0=ALU.mult, op1=ALU.subtract)
        e.tensor_scalar(out=lnV[:], in0=lnV[:], scalar1=EPS_LN, scalar2=None, op0=ALU.add)
        ctx.inc(e.reciprocal(lnV[:], lnV[:]), "lnp_v")
        ctx.wait(e, "lnp_s2", base["lnp_s2"] + 1)
        ctx.inc(e.tensor_tensor(out=lnB[:], in0=lnM[:], in1=lnRn[:], op=ALU.mult), "lnp_v3")

    def ln_prepass_s(e, base):
        for t in range(NT):
            i = e.activation(out=xln_bf[:], in_=resid[:, t, :], func=AF.Square,
                             accum_out=lnQ[:, t:t + 1])
        ctx.inc(i, "lnp_s", 16)
        ctx.wait(e, "lnp_v", base["lnp_v"] + 1)
        e.activation(out=lnR[:], in_=lnV[:], func=AF.Sqrt)
        ctx.inc(e.mul(lnRn[:], lnR[:], -1.0), "lnp_s2")

    def snap(*names):
        return {n: ctx.cnt.get(n, 0) for n in names}

    LNSEMS = ("lnp_s", "lnp_v", "lnp_s2", "lnp_v3")
    for n in LNSEMS:
        ctx.sem(n)

    def emit_l0ffn(tag):
        SAP = tag + "sap"
        TTR = tag + "ttr"
        VEV = tag + "vev"
        TMM = tag + "tmm"
        VEP = tag + "vep"
        base = snap(*LNSEMS)

        with nc.Block() as b:
            @b.vector
            def _(e):
                ln_prepass_v(e, base)
                for t in range(NT):
                    ctx.wait(e, TTR, 8 * (t + 1))
                    ctx.inc(e.tensor_copy(xfm[:], ps_tr[:]), VEV)
                    ctx.wait(e, TMM, 2 * (t + 1))
                    for n in range(2):
                        pm = ps_mm0 if n == 0 else ps_mm1
                        e.scalar_tensor_tensor(out=tsc1[:, 512 * n:512 * (n + 1)], in0=pm[:],
                                               scalar=nhalfalpha[:], in1=halfbias[:, 512 * n:512 * (n + 1)],
                                               op0=ALU.mult, op1=ALU.add)
                    e.tensor_tensor(out=resid[:, t, 0:512], in0=tsc1[:, 0:512],
                                    in1=resid[:, t, 0:512], op=ALU.add)
                    i = e.tensor_tensor(out=resid[:, t, 512:1024], in0=tsc1[:, 512:1024],
                                        in1=resid[:, t, 512:1024], op=ALU.add)
                    ctx.inc(i, VEP)

            @b.scalar
            def _(e):
                ln_prepass_s(e, base)
                ctx.wait(e, "lnp_v3", base["lnp_v3"] + 1)
                for t in range(NT):
                    if t > 0:
                        ctx.wait(e, TTR, 8 * t)
                    ctx.inc(e.activation(out=xln_bf[:], in_=resid[:, t, :], func=AF.Identity,
                                         scale=lnR[:, t:t + 1], bias=lnB[:, t:t + 1]), SAP)

            @b.tensor
            def _(e):
                for t in range(NT):
                    ctx.wait(e, SAP, t + 1)
                    if t > 0:
                        ctx.wait(e, VEV, t)
                    for k in range(8):
                        ctx.inc(e.transpose(ps_tr[:, 128 * k:128 * (k + 1)],
                                            xln_bf[:, 128 * k:128 * (k + 1)], iden[:]), TTR)
                    ctx.wait(e, VEV, t + 1)
                    if t > 0:
                        ctx.wait(e, VEP, t)
                    for n in range(2):
                        pm = ps_mm0 if n == 0 else ps_mm1
                        for k in range(8):
                            i = e.matmul(pm[:], xfm[:, 128 * k:128 * (k + 1)],
                                         ternT[:, k, 512 * n:512 * (n + 1)],
                                         start=(k == 0), stop=(k == 7))
                        ctx.inc(i, TMM)

    emit_l0ffn("f0")

    if _KMAX < 3:
        emit_store(ctx, nc, dout, resid, ST)
        stack.close()
        return nc
    with nc.Block() as b:
        W3 = "p3w"
        T3 = "p3t"
        V3 = "p3v"

        @b.gpsimd
        def _(e):
            for g in range(3):
                ctx.inc(e.dma_start(
                    out=wstage,
                    in_=d_wqkv0[1024 * g:1024 * (g + 1), :].rearrange("(c p) d -> p c d", p=128)), W3, 16)
                ctx.wait(e, T3, 64 * (g + 1))
            ctx.inc(e.dma_start(out=ck, in_=d_ck), W3, 16)
            ctx.inc(e.dma_start(out=sk, in_=d_sk), W3, 16)


        @b.tensor
        def _(e):
            for g in range(3):
                ctx.wait(e, W3, 16 * (g + 1))
                for k in range(8):
                    if not (g == 0 and k == 0):
                        ctx.wait(e, V3, 8 * g + k)
                    for o in range(8):
                        ctx.inc(e.transpose(ps_tr[:, 128 * o:128 * (o + 1)],
                                            wstage[:, o, 128 * k:128 * (k + 1)], iden[:]), T3)

        @b.vector
        def _(e):
            for g in range(3):
                for k in range(8):
                    ctx.wait(e, T3, 64 * g + 8 * (k + 1))
                    ctx.inc(e.tensor_copy(wqkvT[:, k, 1024 * g:1024 * (g + 1)], ps_tr[:]), V3)

    if _KMAX < 4:
        emit_store(ctx, nc, dout, resid, ST)
        stack.close()
        return nc
    with nc.Block() as b:
        base = snap(*LNSEMS)
        SAP = "a1sap"
        TTR = "a1ttr"
        VEV = "a1vev"
        TQK = "a1tqk"
        VRE = "a1vre"
        SEX = "a1sex"
        TPF = "a1tpf"
        VQF = "a1vqf"
        TKV = "a1tkv"

        @b.scalar
        def _(e):
            ln_prepass_s(e, base)
            ctx.wait(e, "lnp_v3", base["lnp_v3"] + 1)
            for t in range(NT):
                if t > 0:
                    ctx.wait(e, TTR, 8 * t)
                ctx.inc(e.activation(out=xln_bf[:], in_=resid[:, t, :], func=AF.Identity,
                                     scale=lnR[:, t:t + 1], bias=lnB[:, t:t + 1]), SAP)
                ctx.wait(e, VRE, 6 * t + 1)
                ctx.inc(e.activation(out=e_e[:], in_=e_m, func=AF.Exp), SEX)
                ctx.wait(e, VRE, 6 * t + 3)
                ctx.inc(e.activation(out=e_e[:], in_=e_m, func=AF.Exp), SEX)

        @b.tensor
        def _(e):
            for t in range(NT):
                ctx.wait(e, SAP, t + 1)
                if t > 0:
                    ctx.wait(e, VQF, t)
                for k in range(8):
                    ctx.inc(e.transpose(ps_tr[:, 128 * k:128 * (k + 1)],
                                        xln_bf[:, 128 * k:128 * (k + 1)], iden[:]), TTR)
                ctx.wait(e, VEV, t + 1)
                for sec in range(6):
                    pm = ps_qa if sec % 2 == 0 else ps_qb
                    w = 6 * t + sec - 1
                    if w > 0:
                        ctx.wait(e, VRE, w)
                    for k in range(8):
                        i = e.matmul(pm[:], xfm[:, 128 * k:128 * (k + 1)],
                                     wqkvT[:, k, 512 * sec:512 * (sec + 1)],
                                     start=(k == 0), stop=(k == 7))
                    ctx.inc(i, TQK)
                ctx.wait(e, VRE, 6 * t + 5)
                if t > 0:
                    ctx.wait(e, VQF, t)
                for pr in range(8):
                    ctx.inc(e.transpose(ps_tr[:, 128 * pr:128 * (pr + 1)],
                                        qftok[:, 128 * pr:128 * (pr + 1)], iden[:]), TPF)
                ctx.wait(e, VRE, 6 * t + 6)
                if t > 0:
                    ctx.wait(e, "a1vkv", 4 * t)
                for h in range(16):
                    i = e.matmul(ps_kv[h // 4][:, 65 * (h % 4):65 * (h % 4) + 65],
                                 kfeat[:, 64 * h:64 * (h + 1)],
                                 vaug[:, 80 * h:80 * h + 65],
                                 start=True, stop=True)
                    ctx.inc(i, TKV)

        @b.vector
        def _(e):
            e.tensor_scalar(out=vaug.rearrange("p (h n) -> p h n", n=80)[:, :, 64:65].squeeze(),
                            in0=consts[:, 4:5].broadcast_to([128, 16]),
                            scalar1=1.0, scalar2=None, op0=ALU.mult)
            ln_prepass_v(e, base)
            for t in range(NT):
                ctx.wait(e, TTR, 8 * (t + 1))
                ctx.inc(e.tensor_copy(xfm[:], ps_tr[:]), VEV)
                ckt = ck[:, 32 * t:32 * (t + 1)].rearrange("p (o n) -> p o n", o=1).broadcast_to([128, 8, 32])
                skt = sk[:, 32 * t:32 * (t + 1)].rearrange("p (o n) -> p o n", o=1).broadcast_to([128, 8, 32])

                def rope(pm, dst):
                    xe = pm[:].rearrange("p (h two f) -> p h two f", two=2, f=32)[:, :, 0, :]
                    xo = pm[:].rearrange("p (h two f) -> p h two f", two=2, f=32)[:, :, 1, :]
                    de = dst.rearrange("p (h two f) -> p h two f", two=2, f=32)[:, :, 0, :]
                    do = dst.rearrange("p (h two f) -> p h two f", two=2, f=32)[:, :, 1, :]
                    rA = ropeA.rearrange("p (h f) -> p h f", f=32)
                    rB = ropeB.rearrange("p (h f) -> p h f", f=32)
                    rC = ropeC.rearrange("p (h f) -> p h f", f=32)
                    rD = ropeD.rearrange("p (h f) -> p h f", f=32)
                    e.tensor_tensor(out=rA, in0=xe, in1=ckt, op=ALU.mult)
                    e.tensor_tensor(out=rB, in0=xo, in1=skt, op=ALU.mult)
                    e.tensor_tensor(out=rC, in0=xe, in1=skt, op=ALU.mult)
                    e.tensor_tensor(out=rD, in0=xo, in1=ckt, op=ALU.mult)
                    e.tensor_tensor(out=de, in0=rA, in1=rB, op=ALU.subtract)
                    e.tensor_tensor(out=do, in0=rC, in1=rD, op=ALU.add)

                ctx.wait(e, TQK, 6 * t + 1)
                rope(ps_qa, qrot[:, 0:512])
                ctx.wait(e, TQK, 6 * t + 2)
                rope(ps_qb, qrot[:, 512:1024])
                e.tensor_scalar(out=e_r, in0=qrot, scalar1=0.125, scalar2=0.0, op0=ALU.mult, op1=ALU.max)
                ctx.inc(e.tensor_scalar(out=e_m, in0=qrot, scalar1=0.125, scalar2=0.0,
                                        op0=ALU.mult, op1=ALU.min), VRE)
                ctx.wait(e, SEX, 2 * t + 1)
                ctx.inc(e.tensor_tensor(out=qftok, in0=e_r, in1=e_e[:], op=ALU.add), VRE)
                ctx.wait(e, TQK, 6 * t + 3)
                rope(ps_qa, qrot[:, 0:512])
                ctx.wait(e, TQK, 6 * t + 4)
                rope(ps_qb, qrot[:, 512:1024])
                e.tensor_scalar(out=e_r, in0=qrot, scalar1=0.0, scalar2=None, op0=ALU.max)
                ctx.inc(e.tensor_scalar(out=e_m, in0=qrot, scalar1=0.0, scalar2=None, op0=ALU.min), VRE)
                ctx.wait(e, TQK, 6 * t + 5)
                va = vaug.rearrange("p (h n) -> p h n", n=80)
                ctx.inc(e.tensor_copy(va[:, 0:8, 0:64], ps_qa[:].rearrange("p (h f) -> p h f", f=64)), VRE)
                ctx.wait(e, TQK, 6 * t + 6)
                ctx.inc(e.tensor_copy(va[:, 8:16, 0:64], ps_qb[:].rearrange("p (h f) -> p h f", f=64)), VRE)
                ctx.wait(e, SEX, 2 * t + 2)
                ctx.inc(e.tensor_tensor(out=kfeat, in0=e_r, in1=e_e[:], op=ALU.add), VRE)
                ctx.wait(e, TPF, 8 * (t + 1))
                ctx.inc(e.tensor_copy(qfeat[:, :, 128 * t:128 * (t + 1)],
                                      ps_tr[:].rearrange("p (pr c) -> p pr c", c=128)), VQF)
                ctx.wait(e, TKV, 16 * (t + 1))
                for g in range(4):
                    ctx.inc(e.tensor_tensor(out=kv_acc[:, 260 * g:260 * (g + 1)],
                                            in0=ps_kv[g], in1=kv_acc[:, 260 * g:260 * (g + 1)],
                                            op=ALU.add), "a1vkv")

    if _KMAX < 5:
        emit_store(ctx, nc, dout, resid, ST)
        stack.close()
        return nc
    with nc.Block() as b:
        V5 = "p5v"
        G5 = "p5g"
        T5 = "p5t"
        W5 = "p5w"

        @b.vector
        def _(e):
            for k in range(8):
                ctx.wait(e, T5, 8 * (k + 1))
                ctx.inc(e.tensor_copy(woutT[:, k, :], ps_tr[:]), V5)

        @b.gpsimd
        def _(e):
            ctx.inc(e.dma_start(out=wstage,
                                in_=d_wout0.rearrange("(c p) d -> p c d", p=128)), W5, 16)
            e.memset(kv_pair, 0.0)
            ctx.inc(e.dma_start(out=cc_kv_in.ap().rearrange("r h n -> r (h n)"), in_=kv_acc[:]), G5, 16)
            ctx.wait(e, G5, 16)
            ctx.inc(e.collective_compute(
                "AllReduce", ALU.add,
                replica_groups=[[0, 1, 2, 3], [4, 5, 6, 7]],
                ins=[cc_kv_in.ap().opt()],
                outs=[cc_kv_out.ap().opt()]), "p5gc", 1)
            ctx.wait(e, "p5gc", 1)
            evens = cc_kv_out.ap().rearrange("r (pr eo) n -> r eo pr n", eo=2)
            ctx.inc(e.dma_start(out=kv_pair[0:64, :, 0:65], in_=evens[:, 0:1, :, :].squeeze()), G5, 16)
            ctx.inc(e.dma_start(out=kv_pair[64:128, :, 80:145], in_=evens[:, 1:2, :, :].squeeze()), G5, 16)

        @b.tensor
        def _(e):
            ctx.wait(e, W5, 16)
            for k in range(8):
                if k > 0:
                    ctx.wait(e, V5, k)
                for o in range(8):
                    ctx.inc(e.transpose(ps_tr[:, 128 * o:128 * (o + 1)],
                                        wstage[:, o, 128 * k:128 * (k + 1)], iden[:]), T5)

    if _KMAX < 6:
        emit_store(ctx, nc, dout, resid, ST)
        stack.close()
        return nc
    KP6 = int(os.environ.get("KP6", "3"))
    TPT6 = {0: 1, 1: 1, 2: 2, 3: 3}[KP6]
    with nc.Block() as b:
        T6 = "a2t"
        V6 = "a2v"
        S6 = "a2s"
        E6 = "a2e"

        @b.tensor
        def _(e):
            ctx.wait(e, "p5g", ctx.cnt["p5g"])
            for t in range(NT):
                if t > 0 and KP6 >= 1:
                    ctx.wait(e, S6, t)
                for pr in range(8):
                    i = e.matmul(ps_qkv[pr // 2][:, 160 * (pr % 2):160 * (pr % 2) + 145],
                                 qfeat[:, pr, 128 * t:128 * (t + 1)],
                                 kv_pair[:, pr, 0:145],
                                 start=True, stop=True)
                ctx.inc(i, T6)
                if KP6 < 2:
                    continue
                ctx.wait(e, S6, t + 1)
                if t > 0:
                    ctx.wait(e, V6, 2 * t)
                for pr in range(8):
                    i = e.transpose(ps_tr[:, 128 * pr:128 * (pr + 1)],
                                    attn_tok[:, 128 * pr:128 * (pr + 1)], iden[:])
                ctx.inc(i, T6)
                if KP6 < 3:
                    continue
                ctx.wait(e, V6, 2 * t + 2)
                if t > 0:
                    ctx.wait(e, E6, t)
                for n in range(2):
                    pm = ps_mm0 if n == 0 else ps_mm1
                    for k in range(8):
                        i = e.matmul(pm[:], afm[:, 128 * k:128 * (k + 1)],
                                     woutT[:, k, 512 * n:512 * (n + 1)],
                                     start=(k == 0), stop=(k == 7))
                ctx.inc(i, T6)

        @b.vector
        def _(e):
            if KP6 < 1:
                return
            for t in range(NT):
                ctx.wait(e, T6, TPT6 * t + 1)
                for g in range(4):
                    e.tensor_scalar(out=dall[:, 4 * g:4 * g + 2],
                                    in0=ps_qkv[g].rearrange("p (pp n) -> p pp n", n=160)[:, :, 64:65].squeeze(),
                                    scalar1=1e-6, scalar2=None, op0=ALU.max)
                    e.tensor_scalar(out=dall[:, 4 * g + 2:4 * g + 4],
                                    in0=ps_qkv[g].rearrange("p (pp n) -> p pp n", n=160)[:, :, 144:145].squeeze(),
                                    scalar1=1e-6, scalar2=None, op0=ALU.max)
                e.drain()
                e.reciprocal(rall, dall)
                ctx.inc(e.drain(), V6)
                if KP6 < 2:
                    ctx.inc(e.drain(), V6)
                    continue
                ctx.wait(e, T6, TPT6 * t + 2)
                ctx.inc(e.tensor_copy(afm, ps_tr[:]), V6)
                if KP6 < 3:
                    continue
                ctx.wait(e, T6, TPT6 * (t + 1))
                e.tensor_tensor(out=resid[:, t, 0:512], in0=ps_mm0[:], in1=resid[:, t, 0:512], op=ALU.add)
                i = e.tensor_tensor(out=resid[:, t, 512:1024], in0=ps_mm1[:], in1=resid[:, t, 512:1024], op=ALU.add)
                ctx.inc(i, E6)

        @b.scalar
        def _(e):
            if KP6 < 1:
                return
            for t in range(NT):
                ctx.wait(e, V6, 2 * t + 1)
                for h in range(16):
                    pr, eo = h // 2, h % 2
                    rc = 4 * (pr // 2) + 2 * eo + (pr % 2)
                    i = e.activation(out=attn_tok[:, 64 * h:64 * (h + 1)],
                                     in_=ps_qkv[pr // 2][:, 160 * (pr % 2) + 80 * eo:160 * (pr % 2) + 80 * eo + 64],
                                     func=AF.Identity, scale=rall[:, rc:rc + 1])
                ctx.inc(i, S6)

    if _KMAX < 7:
        emit_store(ctx, nc, dout, resid, ST)
        stack.close()
        return nc
    emit_l0ffn("f1")

    if _KMAX < 8:
        emit_store(ctx, nc, dout, resid, ST)
        stack.close()
        return nc

    C2Q = 3.0 * 2.0**22
    wgu = [mgbf(16 * b, 8192).rearrange("p (w k n) -> p w k n", w=2, k=8) for b in range(2)]
    wdq = [mgbf(32 + 8 * b, 4096).rearrange("p (k n) -> p k n", k=4) for b in range(2)]
    s_abs = mgf32(64, 2048)
    s_a = mgf32(72, 2048)
    s_q1 = mgf32(80, 2048)
    hid_buf = s_q1
    s_q2 = mgf32(88, 2048)
    qbf = mgbf(88, 2048)
    s_g2 = mgf32(96, 2048)
    sgn_b = mgbf(104, 2048)
    smq = mgf32(108, 1280)
    q_bm = smq[:, 0:256]; q_sc = smq[:, 256:512]; q_cm = smq[:, 512:768]
    q_t1 = smq[:, 768:1024]; q_rec = smq[:, 1024:1280]
    mskq = halfbias[:].bitcast(U8)[:, 0:2048]
    silu_buf2 = mgf32(113, 512)
    xq_fm = mgbf(115, 4096).rearrange("p (k n) -> p k n", k=8)
    hq_fm = mgbf(123, 2048).rearrange("p (k n) -> p k n", k=4)
    down_acc = mgf32(48, 4096)
    gsv = mgf32(134.5, 32)
    wstrip = mgf32(0, 4096)
    mxt = mgf32(108, 640)[0:128, 0:129]
    wqstage = mgbf(16, 4096)

    d_gate_s = nc.dram_tensor("gate_strip", [512, D], F32, kind="ExternalInput").ap()
    d_up_s = nc.dram_tensor("up_strip", [512, D], F32, kind="ExternalInput").ap()
    d_down_s = nc.dram_tensor("down_strip", [D, 512], F32, kind="ExternalInput").ap()
    cc_mx_in = nc.dram_tensor("cc_mx_in", [128, 16], F32)
    cc_mx_out = nc.dram_tensor("cc_mx_out", [128, 16], F32)
    mx_sc = nc.dram_tensor("mx_sc", [16], F32)
    wqg_in = nc.dram_tensor("wqg_in", [3, 128, 4096], BF16)
    wqg_out = nc.dram_tensor("wqg_out", [8, 3, 128, 4096], BF16)

    NBQ = 128

    for n in ("qV", "qS"):
        ctx.sem(n)

    def q_v(e, i, x_ap, gs_ap, gs6_ap):
        x3 = x_ap.rearrange("p (nb b) -> p nb b", b=16)
        e.tensor_reduce(out=q_bm[:, 0:128], in_=x3, axis=AX, op=ALU.max, apply_absolute_value=True)
        e.tensor_scalar(out=q_sc, in0=q_bm, scalar1=gs6_ap, scalar2=448.0, op0=ALU.mult, op1=ALU.min)
        e.tensor_scalar(out=q_cm.bitcast(I32)[:, 0:256], in0=q_sc.bitcast(I32)[:, 0:256], scalar1=0x7F800000,
                        scalar2=None, op0=ALU.bitwise_and)
        e.tensor_scalar(out=q_cm, in0=q_cm, scalar1=1572864.0, scalar2=None, op0=ALU.mult)
        e.tensor_tensor(out=q_t1, in0=q_sc, in1=q_cm, op=ALU.add)
        e.tensor_tensor(out=q_bm, in0=q_t1, in1=q_cm, op=ALU.subtract)
        e.reciprocal(q_rec, q_bm)
        e.tensor_scalar(out=q_rec, in0=q_rec, scalar1=gs_ap, scalar2=1e30, op0=ALU.mult, op1=ALU.min)
        ctx.wait(e, "qS", 2 * i + 1)
        recb = q_rec[:, 0:128].rearrange("p (nb o) -> p nb o", o=1).broadcast_to([128, NBQ, 16])
        ctx.inc(e.tensor_tensor(out=s_a.rearrange("p (nb b) -> p nb b", b=16),
                                in0=s_abs.rearrange("p (nb b) -> p nb b", b=16),
                                in1=recb, op=ALU.mult), "qV")
        e.tensor_scalar(out=s_q1, in0=s_a, scalar1=C_HALF, scalar2=C_HALF, op0=ALU.add, op1=ALU.subtract)
        e.tensor_scalar(out=mskq[:], in0=s_a, scalar1=2.25, scalar2=None, op0=ALU.is_ge)
        e.tensor_scalar(out=s_g2, in0=s_a, scalar1=5.0, scalar2=2.0,
                        op0=ALU.is_ge, op1=ALU.mult)
        ctx.wait(e, "qS", 2 * i + 2)
        e.scalar_tensor_tensor(out=s_abs, in0=s_q2, scalar=4.0, in1=s_g2,
                               op0=ALU.min, op1=ALU.add)
        e.copy_predicated(out=s_q1, mask=mskq[:], data=s_abs)
        scb = q_bm[:, 0:128].rearrange("p (nb o) -> p nb o", o=1).broadcast_to([128, NBQ, 16])
        e.scalar_tensor_tensor(out=s_a.bitcast(BF16)[:, 0:2048].rearrange("p (nb b) -> p nb b", b=16),
                               in0=s_q1.rearrange("p (nb b) -> p nb b", b=16),
                               scalar=6.0, in1=scb, op0=ALU.min, op1=ALU.mult)
        ctx.inc(e.tensor_tensor(out=qbf, in0=s_a.bitcast(BF16)[:, 0:2048],
                                in1=sgn_b, op=ALU.mult), "qV")

    def q_s(e, i, x_ap):
        ctx.wait(e, "qV", 2 * i)
        e.activation(out=s_abs, in_=x_ap, func=AF.Abs)
        ctx.inc(e.activation(out=sgn_b, in_=x_ap, func=AF.Sign), "qS")
        ctx.wait(e, "qV", 2 * i + 1)
        e.activation(out=s_q2, in_=s_a, func=AF.Identity, bias=consts[:, 5:6])
        ctx.inc(e.activation(out=s_q2, in_=s_q2, func=AF.Identity, bias=consts[:, 6:7]), "qS")

    strip_g = mgf32(0, 4096)
    strip_u = mgf32(16, 4096)
    strip_d = mgf32(32, 4096)
    stag_gu = mgbf(48, 4096).rearrange("p (k n) -> p k n", k=8)
    stag_d = mgbf(48, 4096).rearrange("p (k n) -> p k n", k=4)
    with nc.Block() as b:
        W8 = "p8w"
        V8 = "p8v"
        G8 = "p8g"
        T8 = "p8t"
        S8 = "p8s"
        D8 = "p8d"
        QS = [(strip_g, 0, 11, 14), (strip_g, 2048, 11, 14),
              (strip_u, 0, 12, 15), (strip_u, 2048, 12, 15),
              (strip_d, 0, 13, 16), (strip_d, 2048, 13, 16)]
        qbase = ctx.cnt["qV"] // 2

        @b.sync
        def _(e):
            ctx.inc(e.dma_start(out=strip_g.rearrange("p (j d) -> p j d", d=D),
                                in_=d_gate_s.rearrange("(j p) d -> p j d", p=128)), W8, 16)
            ctx.inc(e.dma_start(out=strip_u.rearrange("p (j d) -> p j d", d=D),
                                in_=d_up_s.rearrange("(j p) d -> p j d", p=128)), W8, 16)
            ctx.inc(e.dma_start(out=strip_d.rearrange("p (o i) -> p o i", i=512),
                                in_=d_down_s.rearrange("(o p) i -> p o i", p=128)), W8, 16)
            for w in range(3):
                ctx.wait(e, S8, 4 * (w + 1))
                ctx.inc(e.dma_start(out=wqg_in.ap()[w:w + 1, :, :].squeeze(),
                                    in_=mgbf(48, 4096)), D8, 16)

        @b.vector
        def _(e):
            for w, (st, off) in enumerate([(strip_g, 16), (strip_u, 32), (strip_d, 48)]):
                ctx.wait(e, W8, off)
                ctx.inc(e.tensor_reduce(out=gsv[:, w:w + 1], in_=st, axis=AX, op=ALU.max,
                                        apply_absolute_value=True), V8)
            ctx.wait(e, "p8gz", 1)
            ctx.wait(e, "p8g2", 1)
            ctx.inc(e.tensor_reduce(out=mxt[0:16, 128:129], in_=mxt[0:16, 0:128],
                                    axis=AX, op=ALU.max), "p8g3")
            ctx.wait(e, G8, 16)
            e.drain()
            e.reciprocal(gsv[:, 8:11], gsv[:, 4:7])
            e.reciprocal(gsv[:, 20:22], consts[:, 1:3])
            e.drain()
            e.tensor_scalar(out=gsv[:, 11:14], in0=gsv[:, 8:11], scalar1=2688.0, scalar2=None, op0=ALU.mult)
            e.tensor_scalar(out=gsv[:, 14:17], in0=gsv[:, 8:11], scalar1=448.0, scalar2=None, op0=ALU.mult)
            e.tensor_scalar(out=gsv[:, 26:28], in0=consts[:, 1:3], scalar1=1.0 / 6.0, scalar2=None, op0=ALU.mult)
            e.tensor_scalar(out=gsv[:, 17:18], in0=gsv[:, 4:5], scalar1=gsv[:, 20:21], scalar2=1.0 / 2688.0,
                            op0=ALU.mult, op1=ALU.mult)
            e.tensor_scalar(out=gsv[:, 18:19], in0=gsv[:, 5:6], scalar1=gsv[:, 20:21], scalar2=1.0 / 2688.0,
                            op0=ALU.mult, op1=ALU.mult)
            e.tensor_scalar(out=gsv[:, 19:20], in0=gsv[:, 6:7], scalar1=gsv[:, 21:22], scalar2=0.5 / 2688.0,
                            op0=ALU.mult, op1=ALU.mult)
            ctx.inc(e.drain(), V8)
            for i6 in range(6):
                if i6:
                    ctx.wait(e, T8, 32 * i6)
                st, boff, gcol, g6col = QS[i6]
                q_v(e, qbase + i6, st[:, boff:boff + 2048], gsv[:, gcol:gcol + 1], gsv[:, g6col:g6col + 1])

        @b.scalar
        def _(e):
            ctx.wait(e, V8, 4)
            for i6 in range(6):
                st, boff, _, _ = QS[i6]
                q_s(e, qbase + i6, st[:, boff:boff + 2048])
                for j2 in range(2):
                    ctx.wait(e, T8, 16 * (2 * i6 + j2 + 1))
                    w = i6 // 2
                    if w < 2:
                        j = 2 * (i6 % 2) + j2
                        dst = stag_gu[:, :, 128 * j:128 * (j + 1)]
                        srcv = ps_tr[:].rearrange("p (k n) -> p k n", n=128)
                    else:
                        o = 4 * (i6 % 2) + 2 * j2
                        dst = stag_d[:, :, 128 * o:128 * (o + 2)].rearrange("p k (o2 n) -> p k o2 n", o2=2)
                        srcv = ps_tr[:].rearrange("p (o2 i4 n) -> p i4 o2 n", o2=2, n=128)
                    ctx.inc(e.activation(out=dst, in_=srcv, func=AF.Identity), S8)

        @b.tensor
        def _(e):
            for i6 in range(6):
                ctx.wait(e, "qV", 2 * (qbase + i6 + 1))
                w = i6 // 2
                for j2 in range(2):
                    if not (i6 == 0 and j2 == 0):
                        ctx.wait(e, S8, 2 * i6 + j2)
                    if w < 2:
                        for k in range(8):
                            ctx.inc(e.transpose(ps_tr[:, 128 * k:128 * (k + 1)],
                                                qbf[:, 1024 * j2 + 128 * k:1024 * j2 + 128 * (k + 1)],
                                                iden[:]), T8, 2)
                    else:
                        for o2 in range(2):
                            for i4 in range(4):
                                ctx.inc(e.transpose(ps_tr[:, 512 * o2 + 128 * i4:512 * o2 + 128 * (i4 + 1)],
                                                    qbf[:, 1024 * j2 + 512 * o2 + 128 * i4:
                                                        1024 * j2 + 512 * o2 + 128 * (i4 + 1)],
                                                    iden[:]), T8, 2)

        @b.gpsimd
        def _(e):
            ctx.inc(e.memset(gsv[:, 3:16], 0.0), "p8gz", 1)
            ctx.wait(e, V8, 3)
            ctx.inc(e.dma_start(out=cc_mx_in.ap(), in_=gsv[:, 0:16]), "p8mxd", 16)
            ctx.wait(e, "p8mxd", 16)
            ctx.inc(e.collective_compute(
                "AllReduce", ALU.max,
                replica_groups=[[0, 1, 2, 3, 4, 5, 6, 7]],
                ins=[cc_mx_in.ap().opt()],
                outs=[cc_mx_out.ap().opt()]), "p8mxc", 1)
            ctx.wait(e, "p8mxc", 1)
            with nc.allow_non_contiguous_dma(reason="one-time 8KB gmax transpose"):
                ctx.inc(e.dma_start(out=mxt[0:16, 0:128], in_=cc_mx_out.ap().transpose([1, 0])), "p8mxd", 16)
            ctx.wait(e, "p8mxd", 32)
            ctx.inc(e.drain(), "p8g2", 1)
            ctx.wait(e, "p8g3", 1)
            ctx.inc(e.dma_start(out=mx_sc.ap(), in_=mxt[0:16, 128:129]), "p8mxd", 16)
            ctx.wait(e, "p8mxd", 48)
            ctx.inc(e.dma_start(out=gsv[:, 4:20],
                                in_=mx_sc.ap().rearrange("(o n) -> o n", o=1).broadcast_to([128, 16])),
                    G8, 16)
            ctx.wait(e, D8, 48)
            ctx.inc(e.collective_compute(
                "AllGather", ALU.bypass,
                replica_groups=[[0, 1, 2, 3, 4, 5, 6, 7]],
                ins=[wqg_in.ap().opt()],
                outs=[wqg_out.ap().opt()]), "p8wgc", 1)
            ctx.wait(e, "p8wgc", 1)

    if _KMAX < 9:
        emit_store(ctx, nc, dout, resid, ST)
        stack.close()
        return nc

    silu_buf = silu_buf2

    def emit_l1ffn(tag):
        base = snap(*LNSEMS)
        SAP = tag + "sap"
        TXQ = tag + "txq"
        SEV = tag + "sev"
        WGU = tag + "wgu"
        WDQ = tag + "wdq"
        TGU = tag + "tgu"
        SSG = tag + "ssg"
        VHD = tag + "vhd"
        THQ = tag + "thq"
        TDN = tag + "tdn"
        ACC = tag + "acc"
        qb0 = ctx.cnt["qV"] // 2

        def qidx(s, j):
            return qb0 + 10 * s + j

        with nc.Block() as b:
            @b.sync
            def _(e):
                for s in range(4):
                    for ei in range(8):
                        E = 8 * s + ei
                        pb = E % 2
                        if E >= 2:
                            ctx.wait(e, TGU, 8 * (E - 1))
                        ctx.inc(e.dma_start(out=wgu[pb][:, 0, :, :],
                                            in_=wqg_out.ap()[ei:ei + 1, 0:1, :, :].squeeze()), WGU, 16)
                        ctx.inc(e.dma_start(out=wgu[pb][:, 1, :, :],
                                            in_=wqg_out.ap()[ei:ei + 1, 1:2, :, :].squeeze()), WGU, 16)
                        if E >= 2:
                            ctx.wait(e, TDN, 8 * (E - 1))
                        ctx.inc(e.dma_start(out=wdq[pb][:, :, :],
                                            in_=wqg_out.ap()[ei:ei + 1, 2:3, :, :].squeeze()), WDQ, 16)

            @b.scalar
            def _(e):
                ln_prepass_s(e, base)
                ctx.wait(e, "lnp_v3", base["lnp_v3"] + 1)
                for s in range(4):
                    for h in range(2):
                        q = qidx(s, h)
                        t0 = 4 * s + 2 * h
                        ctx.wait(e, "qV", 2 * q)
                        for tt2 in range(2):
                            ctx.inc(e.activation(out=hid_buf[:, 1024 * tt2:1024 * (tt2 + 1)],
                                                 in_=resid[:, t0 + tt2, :], func=AF.Identity,
                                                 scale=lnR[:, t0 + tt2:t0 + tt2 + 1],
                                                 bias=lnB[:, t0 + tt2:t0 + tt2 + 1]), SAP)
                        q_s(e, q, hid_buf)
                        for tt2 in range(2):
                            g4 = 2 * h + tt2
                            ctx.wait(e, TXQ, 32 * s + 8 * (g4 + 1))
                            ctx.inc(e.activation(out=xq_fm[:, :, 128 * g4:128 * g4 + 128],
                                                 in_=ps_tr[:].rearrange("p (k n) -> p k n", n=128),
                                                 func=AF.Identity), SEV)
                    for ei in range(8):
                        E = 8 * s + ei
                        q = qidx(s, 2 + ei)
                        for tt in range(4):
                            g = 4 * E + tt
                            ctx.wait(e, TGU, 2 * g + 1)
                            if g > 0:
                                ctx.wait(e, VHD, g)
                            ctx.inc(e.activation(out=silu_buf, in_=psA[0][:, 0:512], func=AF.Silu,
                                                 scale=gsv[:, 17:18]), SSG)
                        ctx.wait(e, VHD, 4 * (E + 1))
                        q_s(e, q, hid_buf)
                        for tt in range(4):
                            ctx.wait(e, THQ, 16 * E + 4 * (tt + 1))
                            ctx.inc(e.activation(out=hq_fm[:, :, 128 * tt:128 * (tt + 1)],
                                                 in_=ps_tr[:, 0:512].rearrange("p (k n) -> p k n", n=128),
                                                 func=AF.Identity), SEV)
                        if ei == 0:
                            for tt in range(4):
                                d = 4 * E + tt
                                ctx.wait(e, TDN, 2 * d + 1)
                                ctx.inc(e.activation(out=down_acc[:, 1024 * tt:1024 * tt + 512],
                                                     in_=psA[2][:, 0:512], func=AF.Identity), ACC)
                                ctx.wait(e, TDN, 2 * d + 2)
                                ctx.inc(e.activation(out=down_acc[:, 1024 * tt + 512:1024 * (tt + 1)],
                                                     in_=psA[3][:, 0:512], func=AF.Identity), ACC)

            @b.vector
            def _(e):
                ln_prepass_v(e, base)
                for s in range(4):
                    for h in range(2):
                        q = qidx(s, h)
                        ctx.wait(e, SAP, 4 * s + 2 * h + 2)
                        if h == 1:
                            ctx.wait(e, TXQ, 32 * s + 16)
                        elif s > 0:
                            ctx.wait(e, THQ, 16 * 8 * s)
                        q_v(e, q, hid_buf, consts[:, 1:2], gsv[:, 26:27])
                    for ei in range(8):
                        E = 8 * s + ei
                        q = qidx(s, 2 + ei)
                        ctx.wait(e, "qS", 2 * (q - 1) + 1)
                        if ei == 0:
                            ctx.wait(e, TXQ, 32 * (s + 1))
                        else:
                            ctx.wait(e, THQ, 16 * E)
                        for tt in range(4):
                            g = 4 * E + tt
                            ctx.wait(e, TGU, 2 * g + 2)
                            ctx.wait(e, SSG, g + 1)
                            ctx.inc(e.scalar_tensor_tensor(out=hid_buf[:, 512 * tt:512 * (tt + 1)],
                                                           in0=psA[1][:, 0:512], scalar=gsv[:, 18:19],
                                                           in1=silu_buf, op0=ALU.mult, op1=ALU.mult), VHD)
                        q_v(e, q, hid_buf, consts[:, 2:3], gsv[:, 27:28])
                        if ei > 0:
                            for tt in range(4):
                                d = 4 * E + tt
                                if ei < 7:
                                    ctx.wait(e, TDN, 2 * d + 1)
                                    ctx.inc(e.tensor_tensor(out=down_acc[:, 1024 * tt:1024 * tt + 512],
                                                            in0=psA[2][:, 0:512],
                                                            in1=down_acc[:, 1024 * tt:1024 * tt + 512],
                                                            op=ALU.add), ACC)
                                    ctx.wait(e, TDN, 2 * d + 2)
                                    ctx.inc(e.tensor_tensor(out=down_acc[:, 1024 * tt + 512:1024 * (tt + 1)],
                                                            in0=psA[3][:, 0:512],
                                                            in1=down_acc[:, 1024 * tt + 512:1024 * (tt + 1)],
                                                            op=ALU.add), ACC)
                                else:
                                    t = 4 * s + tt
                                    for nh in range(2):
                                        ctx.wait(e, TDN, 2 * d + 1 + nh)
                                        asl = down_acc[:, 1024 * tt + 512 * nh:1024 * tt + 512 * (nh + 1)]
                                        e.tensor_tensor(out=asl, in0=psA[2 + nh][:, 0:512], in1=asl, op=ALU.add)
                                        ctx.inc(e.scalar_tensor_tensor(
                                            out=resid[:, t, 512 * nh:512 * (nh + 1)],
                                            in0=asl, scalar=gsv[:, 19:20],
                                            in1=resid[:, t, 512 * nh:512 * (nh + 1)],
                                            op0=ALU.mult, op1=ALU.add), ACC)

            @b.tensor
            def _(e):
                nev = 0
                for s in range(4):
                    for g4 in range(4):
                        h, tt2 = g4 // 2, g4 % 2
                        q = qidx(s, h)
                        ctx.wait(e, "qV", 2 * (q + 1))
                        if nev:
                            ctx.wait(e, SEV, nev)
                        nev += 1
                        for k in range(8):
                            ctx.inc(e.transpose(ps_tr[:, 128 * k:128 * (k + 1)],
                                                qbf[:, 1024 * tt2 + 128 * k:1024 * tt2 + 128 * (k + 1)],
                                                iden[:]), TXQ)
                    for ei in range(8):
                        E = 8 * s + ei
                        q = qidx(s, 2 + ei)
                        for tt in range(4):
                            g = 4 * E + tt
                            ctx.wait(e, SEV, 36 * s + 4)
                            ctx.wait(e, WGU, 32 * (E + 1))
                            if g > 0:
                                ctx.wait(e, SSG, g)
                                ctx.wait(e, VHD, g)
                            for k in range(8):
                                i = e.matmul(psA[0][:, 0:512], xq_fm[:, k, 128 * tt:128 * (tt + 1)],
                                             wgu[E % 2][:, 0, k, :], start=(k == 0), stop=(k == 7))
                            ctx.inc(i, TGU)
                            for k in range(8):
                                i = e.matmul(psA[1][:, 0:512], xq_fm[:, k, 128 * tt:128 * (tt + 1)],
                                             wgu[E % 2][:, 1, k, :], start=(k == 0), stop=(k == 7))
                            ctx.inc(i, TGU)
                        ctx.wait(e, "qV", 2 * (q + 1))
                        for tt in range(4):
                            if nev:
                                ctx.wait(e, SEV, nev)
                            nev += 1
                            for i4 in range(4):
                                ctx.inc(e.transpose(ps_tr[:, 128 * i4:128 * (i4 + 1)],
                                                    qbf[:, 512 * tt + 128 * i4:512 * tt + 128 * (i4 + 1)],
                                                    iden[:]), THQ)
                        ctx.wait(e, WDQ, 16 * (E + 1))
                        for tt in range(4):
                            d = 4 * E + tt
                            ctx.wait(e, SEV, 36 * s + 4 + 4 * ei + tt + 1)
                            if d > 0:
                                ctx.wait(e, ACC, 2 * d)
                            for nh in range(2):
                                for i4 in range(4):
                                    i = e.matmul(psA[2 + nh][:, 0:512],
                                                 hq_fm[:, i4, 128 * tt:128 * (tt + 1)],
                                                 wdq[E % 2][:, i4, 512 * nh:512 * (nh + 1)],
                                                 start=(i4 == 0), stop=(i4 == 3))
                                ctx.inc(i, TDN)

    emit_l1ffn("f2")

    if _KMAX < 10:
        emit_store(ctx, nc, dout, resid, ST)
        stack.close()
        return nc

    d_wqkv1 = nc.dram_tensor("L1_attn_w_qkv", [D, 3 * D], F32, kind="ExternalInput").ap()
    d_proj1 = nc.dram_tensor("L1_attn_proj", [128, 64], F32, kind="ExternalInput").ap()
    d_wout1 = nc.dram_tensor("L1_attn_w_out", [D, D], F32, kind="ExternalInput").ap()
    d_bout1 = nc.dram_tensor("bout1_b", [128, D], F32, kind="ExternalInput").ap()
    d_mm_sh = nc.dram_tensor("mm_sh", [128, 128], F32, kind="ExternalInput").ap()
    d_mm_t0 = nc.dram_tensor("mm_t0", [128, 128], F32, kind="ExternalInput").ap()
    d_me = nc.dram_tensor("me3", [6, 256], F32, kind="ExternalInput").ap()
    d_wsl = nc.dram_tensor("wsel_l", [128, 8], BF16, kind="ExternalInput").ap()
    d_wsr = nc.dram_tensor("wsel_r", [128, 8], BF16, kind="ExternalInput").ap()
    cc_hl_in = nc.dram_tensor("cc_hl_in", [4656], BF16)
    cc_hl_out = nc.dram_tensor("cc_hl_out", [8, 4656], BF16)

    wq_hg = mgbf(0, 12288).rearrange("p (k n) -> p k n", k=8)
    S_sb = mgbf(0, 4096).rearrange("p (i n) -> p i n", i=32)
    SE_sb = mgbf(8, 4096).rearrange("p (i n) -> p i n", i=32)
    wout1 = mgbf(0, 8192).rearrange("p (h n) -> p h n", h=8)
    x_fm1 = mgbf(24, 16384).rearrange("p (k n) -> p k n", k=8)
    afm1 = mgbf(56, 16384).rearrange("p (h n) -> p h n", h=8)
    v_s4 = mgbf(88, 9792).rearrange("p (h t n) -> p h t n", h=4, t=17)
    qh_t4 = mgbf(107.25, 512).rearrange("p (h n) -> p h n", h=4)
    qh_fl = mgbf(107.25, 512)
    attn_tok2 = mgbf(107.25, 512)
    kphi1 = mgbf(108.5, 4128).rearrange("p (r n) -> p r n", r=2)
    qphi1 = mgbf(116.75, 4128).rearrange("p (r n) -> p r n", r=2)
    elu_r = mgf32(125, 512)
    elu_m = mgf32(127, 512)
    hl_v1 = mgbf(125, 1032)
    hl_k2 = mgbf(127.25, 96)
    k_stg = mgbf(129, 512).rearrange("p (s r n) -> p s r n", s=2, r=2)
    q_stg = mgbf(130, 512).rearrange("p (s r n) -> p s r n", s=2, r=2)
    v_aln = mgbf(131, 1024).rearrange("p (s h n) -> p s h n", s=2, h=4)
    v_aln_fl = mgbf(131, 1024)
    hs_tmp = mgbf(133, 516)
    proj_s = lnM[:].bitcast(BF16)[:, 0:64]
    wsel_l = lnM[:].bitcast(BF16)[:, 64:72]
    wsel_r = lnM[:].bitcast(BF16)[:, 72:80]
    mm_sh = lnV[:]
    mm_t0 = lnRn[:]
    me_std = lnR[0:6, 0:128]
    me_t15 = lnB[0:6, 0:128]
    zsc = lnS[:][:, 32:48]
    bout1 = halfbias

    AS = ("awq", "atqk", "asex", "asev", "avel", "agrl", "agva", "agdm",
          "agcc", "avhl", "atsm", "avsm", "atat", "avz", "asat", "attr", "aghd")
    for _n in AS:
        ctx.sem(_n)
    (WQ3, TQK, SEX, SEV, VEL, GRL, GVA, GDM,
     GCC, VHL, TSM, VSM, TAT, VZ, SAT, TTR2, GHD) = AS
    KF, KL, VF, VL = 0, 768, 1536, 3084

    def emit_l1attn_hg(hg):
        bs = snap(*AS)

        with nc.Block() as b:
            @b.gpsimd
            def _(e):
                if hg == 0:
                    ctx.inc(e.dma_start(out=proj_s, in_=d_proj1), WQ3, 16)
                    ctx.inc(e.dma_start(out=wsel_l, in_=d_wsl), WQ3, 16)
                    ctx.inc(e.dma_start(out=wsel_r, in_=d_wsr), WQ3, 16)
                    ctx.inc(e.dma_start(out=mm_sh, in_=d_mm_sh), WQ3, 16)
                    ctx.inc(e.dma_start(out=mm_t0, in_=d_mm_t0), WQ3, 16)
                    ctx.inc(e.dma_start(out=me_std, in_=d_me[:, 0:128]), WQ3, 16)
                    ctx.inc(e.dma_start(out=me_t15, in_=d_me[:, 128:256]), WQ3, 16)
                for w3 in range(3):
                    ctx.inc(e.dma_start(
                        out=wq_hg[:, :, 512 * w3:512 * (w3 + 1)],
                        in_=d_wqkv1[:, 1024 * w3 + 512 * hg:1024 * w3 + 512 * (hg + 1)]
                        .rearrange("(k p) n -> p k n", p=128)), WQ3, 16)
                for t in range(NT):
                    ctx.wait(e, SEV, bs[SEV] + 3 * t + 3)
                    ctx.inc(e.dma_start(out=v_s4[3:128, :, t, 0:128],
                                        in_=v_aln[0:125, t % 2, :, :]), GVA, 16)
                    ctx.inc(e.dma_start(out=v_s4[0:3, :, t + 1, 0:128],
                                        in_=v_aln[125:128, t % 2, :, :]), GVA, 16)
                    ctx.wait(e, VEL, bs[VEL] + 6 * t + 3)
                    ctx.inc(e.dma_start(out=qphi1[64:128, :, 128 * t:128 * t + 128],
                                        in_=q_stg[0:64, t % 2, :, :]), GRL, 16)
                    ctx.wait(e, VEL, bs[VEL] + 6 * t + 6)
                    ctx.inc(e.dma_start(out=kphi1[64:128, :, 128 * t + 3:128 * t + 131],
                                        in_=k_stg[0:64, t % 2, :, :]), GRL, 16)
                ctx.wait(e, GRL, bs[GRL] + 16 * 32)
                ctx.wait(e, GVA, bs[GVA] + 16 * 32)
                cin = cc_hl_in.ap()
                ctx.inc(e.dma_start(out=cin[KF:KF + 768].rearrange("(p r n) -> p r n", p=128, r=2),
                                    in_=kphi1[:, :, 3:6]), GDM, 16)
                ctx.inc(e.dma_start(out=cin[KL:KL + 768].rearrange("(p r n) -> p r n", p=128, r=2),
                                    in_=kphi1[:, :, 2048:2051]), GDM, 16)
                ctx.inc(e.dma_start(out=cin[VF:VF + 1548].rearrange("(p h n) -> p h n", p=3, h=4),
                                    in_=v_s4[3:6, :, 0, 0:129]), GDM, 16)
                ctx.inc(e.dma_start(out=cin[VL:VL + 1548].rearrange("(p h n) -> p h n", p=3, h=4),
                                    in_=v_s4[0:3, :, 16, 0:129]), GDM, 16)
                ctx.wait(e, GDM, bs[GDM] + 64)
                ctx.inc(e.collective_compute(
                    "AllGather", ALU.bypass,
                    replica_groups=[[0, 1, 2, 3, 4, 5, 6, 7]],
                    ins=[cc_hl_in.ap().opt()],
                    outs=[cc_hl_out.ap().opt()]), GCC, 1)
                ctx.wait(e, GCC, bs[GCC] + 1)
                flat = cc_hl_out.ap()
                ctx.inc(e.dma_start(out=hl_k2[:, 0:48].rearrange("p (r n) -> p r n", r=8),
                                    in_=flat[:, KL:KL + 768].rearrange("r (p n) -> p r n", p=128)),
                        GDM, 16)
                ctx.inc(e.dma_start(out=hl_k2[:, 48:96].rearrange("p (r n) -> p r n", r=8),
                                    in_=flat[:, KF:KF + 768].rearrange("r (p n) -> p r n", p=128)),
                        GDM, 16)
                for si in range(2):
                    voff = VL if si == 0 else VF
                    for hl in range(4):
                        vidx = 4 * si + hl
                        if vidx > 0:
                            ctx.wait(e, VHL, bs[VHL] + 2 + vidx)
                        ctx.inc(e.dma_start(
                            out=hl_v1[0:3, :].rearrange("p (r n) -> p r n", r=8),
                            in_=flat[:, voff:voff + 1548]
                            .rearrange("r (p h n) -> p h r n", p=3, h=4)[:, hl, :, :]),
                            GDM, 16)
                    ctx.wait(e, VHL, bs[VHL] + 2 + 4 * (si + 1))
                    dstv = (v_s4[0:3, :, 0, 0:129] if si == 0
                            else v_s4[3:6, :, 16, 0:129])
                    ctx.inc(e.dma_start(
                        out=dstv, in_=hs_tmp[0:3, :].rearrange("p (h n) -> p h n", h=4)),
                        GHD, 16)

            wql = ctx.cnt[WQ3]

            @b.scalar
            def _(e):
                for t in range(NT):
                    for w in range(2):
                        ctx.wait(e, TQK, bs[TQK] + 5 * t + 2 * w + 1)
                        ctx.inc(e.activation(out=qh_fl, in_=psA[0][:, 0:512],
                                             func=AF.Identity), SEV)
                        ctx.wait(e, VEL, bs[VEL] + 6 * t + 3 * w + 1)
                        ctx.inc(e.activation(out=psA[1][0:64, 0:512], in_=elu_m[0:64, :],
                                             func=AF.Exp), SEX)
                    ctx.wait(e, TQK, bs[TQK] + 5 * t + 5)
                    if t >= 2:
                        ctx.wait(e, GVA, bs[GVA] + 32 * (t - 1))
                    ctx.inc(e.activation(out=v_aln_fl[:, 512 * (t % 2):512 * (t % 2) + 512],
                                         in_=psA[2][:, 0:512], func=AF.Identity), SEV)
                for eo in range(2):
                    for t in range(NT):
                        ctx.wait(e, VZ, bs[VZ] + 48 * eo + 3 * t + 2)
                        bank = psA[3 + t % 2]
                        for prl in range(2):
                            ctx.inc(e.activation(out=attn_tok2[:, 256 * prl:256 * prl + 128],
                                                 in_=bank[:, 160 * prl:160 * prl + 128],
                                                 func=AF.Identity,
                                                 scale=zsc[:, prl:prl + 1]), SAT)

            @b.vector
            def _(e):
                for t in range(NT):
                    for w in range(2):
                        ctx.wait(e, TQK, bs[TQK] + 5 * t + 2 * w + 2)
                        e.tensor_scalar(out=elu_r[0:64, :], in0=psA[1][0:64, 0:512],
                                        scalar1=0.0, scalar2=None, op0=ALU.max)
                        ctx.inc(e.tensor_scalar(out=elu_m[0:64, :], in0=psA[1][0:64, 0:512],
                                                scalar1=0.0, scalar2=None, op0=ALU.min), VEL)
                        ctx.wait(e, SEX, bs[SEX] + 2 * t + w + 1)
                        if t >= 2:
                            ctx.wait(e, GRL, bs[GRL] + 16 * (2 * (t - 2) + 1 + w))
                        for hl in range(4):
                            prl, eo = hl // 2, hl % 2
                            if eo == 0:
                                if w == 0:
                                    dst = qphi1[0:64, prl, 128 * t:128 * t + 128]
                                else:
                                    dst = kphi1[0:64, prl, 128 * t + 3:128 * t + 131]
                            else:
                                stg = q_stg if w == 0 else k_stg
                                dst = stg[0:64, t % 2, prl, :]
                            i = e.tensor_tensor(out=dst, in0=elu_r[0:64, 128 * hl:128 * (hl + 1)],
                                                in1=psA[1][0:64, 128 * hl:128 * (hl + 1)],
                                                op=ALU.add)
                        ctx.inc(i, VEL)
                        ctx.inc(e.drain(), VEL)
                ctx.wait(e, WQ3, wql)
                ctx.wait(e, GDM, bs[GDM] + 96)
                for (cl, wsl, dstk) in ((0, wsel_l, kphi1[:, :, 0:3]),
                                        (48, wsel_r, kphi1[:, :, 2051:2054])):
                    hv = hl_k2[:, cl:cl + 48].rearrange("p (r n) -> p n r", r=8)
                    wb = wsl.rearrange("p (o r) -> p o r", o=1).broadcast_to([128, 6, 8])
                    e.tensor_tensor(out=hv, in0=hv, in1=wb, op=ALU.mult)
                    e.drain()
                    with nc.allow_low_precision(reason="one-hot rank select, exact in bf16"):
                        e.tensor_reduce(out=dstk, in_=hv, axis=AX, op=ALU.add)
                    ctx.inc(e.drain(), VHL)
                for si in range(2):
                    wsl = wsel_l if si == 0 else wsel_r
                    for hl in range(4):
                        vidx = 4 * si + hl
                        ctx.wait(e, GDM, bs[GDM] + 96 + 16 * (vidx + 1))
                        if vidx == 4:
                            ctx.wait(e, GHD, bs[GHD] + 16)
                        hv = hl_v1[0:3, :].rearrange("p (r n) -> p n r", r=8)
                        wb = wsl[0:3, :].rearrange("p (o r) -> p o r", o=1)\
                            .broadcast_to([3, 129, 8])
                        e.tensor_tensor(out=hv, in0=hv, in1=wb, op=ALU.mult)
                        e.drain()
                        dst = hs_tmp[0:3, 129 * hl:129 * (hl + 1)]
                        with nc.allow_low_precision(reason="one-hot rank select, exact in bf16"):
                            e.tensor_reduce(out=dst, in_=hv, axis=AX, op=ALU.add)
                        ctx.inc(e.drain(), VHL)
                for eo in range(2):
                    for t in range(NT):
                        for prl in range(2):
                            gi = 2 * t + prl
                            ctx.wait(e, TSM, bs[TSM] + 64 * eo + 2 * gi + 2)
                            pm = psA[gi % 2]
                            mmask = mm_t0 if t == 0 else mm_sh
                            e.tensor_tensor(out=S_sb[:, gi, :], in0=pm[:, 0:128],
                                            in1=mmask, op=ALU.mult)
                            emask = me_t15 if t == NT - 1 else me_std
                            ctx.inc(e.tensor_tensor(out=SE_sb[0:6, gi, :], in0=pm[0:6, 128:256],
                                                    in1=emask, op=ALU.mult), VSM)
                    for t in range(NT):
                        ctx.wait(e, TAT, bs[TAT] + 32 * eo + 2 * (t + 1))
                        if eo or t:
                            ctx.wait(e, SAT, bs[SAT] + 32 * eo + 2 * t)
                        bank = psA[3 + t % 2]
                        e.tensor_scalar(out=zsc[:, 0:1], in0=bank[:, 128:129],
                                        scalar1=1e-8, scalar2=None, op0=ALU.add)
                        e.tensor_scalar(out=zsc[:, 1:2], in0=bank[:, 288:289],
                                        scalar1=1e-8, scalar2=None, op0=ALU.add)
                        e.drain()
                        e.reciprocal(zsc[:, 0:2], zsc[:, 0:2])
                        ctx.inc(e.drain(), VZ, 2)
                        ctx.wait(e, TTR2, bs[TTR2] + 32 * eo + 2 * (t + 1))
                        e.tensor_copy(afm1[:, 4 * hg + eo, 128 * t:128 * (t + 1)],
                                      ps_tr[:, 0:128])
                        ctx.inc(e.tensor_copy(afm1[:, 4 * hg + 2 + eo, 128 * t:128 * (t + 1)],
                                              ps_tr[:, 128:256]), VZ)

            @b.tensor
            def _(e):
                ctx.wait(e, WQ3, wql)
                for t in range(NT):
                    for w in range(2):
                        if t or w:
                            ctx.wait(e, SEV, bs[SEV] + 3 * t + w)
                        for hl in range(4):
                            for k in range(8):
                                i = e.matmul(psA[0][:, 128 * hl:128 * (hl + 1)],
                                             wq_hg[:, k, 512 * w + 128 * hl:512 * w + 128 * (hl + 1)],
                                             x_fm1[:, k, 128 * t:128 * (t + 1)],
                                             start=(k == 0), stop=(k == 7))
                        ctx.inc(i, TQK)
                        ctx.wait(e, SEV, bs[SEV] + 3 * t + w + 1)
                        if t or w:
                            ctx.wait(e, VEL, bs[VEL] + 6 * t + 3 * w - 1)
                        for hl in range(4):
                            i = e.matmul(psA[1][0:64, 128 * hl:128 * (hl + 1)], proj_s,
                                         qh_t4[:, hl, :], start=True, stop=True)
                        ctx.inc(i, TQK)
                    if t > 0:
                        ctx.wait(e, SEV, bs[SEV] + 3 * t)
                    for k in range(8):
                        i = e.matmul(psA[2][:, 0:512], x_fm1[:, k, 128 * t:128 * (t + 1)],
                                     wq_hg[:, k, 1024:1536], start=(k == 0), stop=(k == 7))
                    ctx.inc(i, TQK)
                ctx.wait(e, VHL, bs[VHL] + 10)
                ctx.wait(e, GHD, bs[GHD] + 32)
                for eo in range(2):
                    kb = 64 * eo
                    for t in range(NT):
                        for prl in range(2):
                            gi = 2 * t + prl
                            pm = psA[gi % 2]
                            if 32 * eo + gi >= 2:
                                ctx.wait(e, VSM, bs[VSM] + 32 * eo + gi - 1)
                            qc = qphi1[kb:kb + 64, prl, 128 * t:128 * t + 128]
                            e.matmul(pm[:, 0:128],
                                     kphi1[kb:kb + 64, prl, 128 * t:128 * t + 128],
                                     qc, start=True, stop=True)
                            i = e.matmul(pm[0:6, 128:256],
                                         kphi1[kb:kb + 64, prl, 128 * t + 128:128 * t + 134],
                                         qc, start=True, stop=True)
                            ctx.inc(i, TSM, 2)
                    ctx.wait(e, VSM, bs[VSM] + 32 * (eo + 1))
                    for t in range(NT):
                        if t >= 2:
                            ctx.wait(e, SAT, bs[SAT] + 32 * eo + 2 * (t - 1))
                        elif eo == 1:
                            ctx.wait(e, SAT, bs[SAT] + 2 * (14 + t) + 2)
                        bank = psA[3 + t % 2]
                        for prl in range(2):
                            gi = 2 * t + prl
                            hl = 2 * prl + eo
                            out_ = bank[:, 160 * prl:160 * prl + 129]
                            e.matmul(out_, S_sb[:, gi, :], v_s4[:, hl, t, 0:129],
                                     start=True, stop=False)
                            i = e.matmul(out_, SE_sb[0:6, gi, :], v_s4[0:6, hl, t + 1, 0:129],
                                         start=False, stop=True)
                        ctx.inc(i, TAT, 2)
                        if t >= 1:
                            tt = t - 1
                            ctx.wait(e, SAT, bs[SAT] + 32 * eo + 2 * (tt + 1))
                            if eo or tt:
                                ctx.wait(e, VZ, bs[VZ] + 48 * eo + 3 * tt)
                            e.transpose(ps_tr[:, 0:128], attn_tok2[:, 0:128], iden[:])
                            i = e.transpose(ps_tr[:, 128:256], attn_tok2[:, 256:384], iden[:])
                            ctx.inc(i, TTR2, 2)
                    tt = NT - 1
                    ctx.wait(e, SAT, bs[SAT] + 32 * eo + 2 * (tt + 1))
                    ctx.wait(e, VZ, bs[VZ] + 48 * eo + 3 * tt)
                    e.transpose(ps_tr[:, 0:128], attn_tok2[:, 0:128], iden[:])
                    i = e.transpose(ps_tr[:, 128:256], attn_tok2[:, 256:384], iden[:])
                    ctx.inc(i, TTR2, 2)

    def emit_l1attn():
        base = snap(*LNSEMS)
        SAP = "a3sap"
        TTR = "a3ttr"
        VEV = "a3vev"
        LD3 = "a3ld"

        with nc.Block() as b:
            @b.sync
            def _(e):
                ctx.inc(e.dma_start(out=bout1[:], in_=d_bout1), LD3, 16)

            @b.scalar
            def _(e):
                ln_prepass_s(e, base)
                ctx.wait(e, "lnp_v3", base["lnp_v3"] + 1)
                for t in range(NT):
                    if t > 0:
                        ctx.wait(e, TTR, 8 * t)
                    ctx.inc(e.activation(out=xln_bf[:], in_=resid[:, t, :], func=AF.Identity,
                                         scale=lnR[:, t:t + 1], bias=lnB[:, t:t + 1]), SAP)

            @b.tensor
            def _(e):
                for t in range(NT):
                    ctx.wait(e, SAP, t + 1)
                    if t > 0:
                        ctx.wait(e, VEV, t)
                    for k in range(8):
                        ctx.inc(e.transpose(ps_tr[:, 128 * k:128 * (k + 1)],
                                            xln_bf[:, 128 * k:128 * (k + 1)], iden[:]), TTR)

            @b.vector
            def _(e):
                ln_prepass_v(e, base)
                one1 = consts[:, 4:5].rearrange("p (a o) -> p a o", o=1)
                for h in range(4):
                    e.tensor_tensor(out=v_s4[:, h, :, 128:129],
                                    in0=one1.broadcast_to([128, 17, 1]),
                                    in1=one1.broadcast_to([128, 17, 1]), op=ALU.mult)
                for t in range(NT):
                    ctx.wait(e, TTR, 8 * (t + 1))
                    ctx.inc(e.tensor_copy(x_fm1[:, :, 128 * t:128 * (t + 1)],
                                          ps_tr[:].rearrange("p (k n) -> p k n", n=128)), VEV)

        for hg in range(2):
            emit_l1attn_hg(hg)

        T4 = "a3t4"
        V4 = "a3v4"
        with nc.Block() as b:
            @b.gpsimd
            def _(e):
                ctx.inc(e.dma_start(out=wout1,
                                    in_=d_wout1.rearrange("(h p) n -> p h n", p=128)), "a3ld4", 16)
            ld4 = ctx.cnt["a3ld4"]

            @b.tensor
            def _(e):
                ctx.wait(e, "a3ld4", ld4)
                for t in range(NT):
                    if t >= 2:
                        ctx.wait(e, V4, t - 1)
                    for n in range(2):
                        pm = psA[2 + 2 * (t % 2) + n]
                        for h in range(8):
                            i = e.matmul(pm[:, 0:512], afm1[:, h, 128 * t:128 * (t + 1)],
                                         wout1[:, h, 512 * n:512 * (n + 1)],
                                         start=(h == 0), stop=(h == 7))
                    ctx.inc(i, T4)

            @b.vector
            def _(e):
                for t in range(NT):
                    ctx.wait(e, T4, t + 1)
                    for n in range(2):
                        pm = psA[2 + 2 * (t % 2) + n]
                        e.tensor_tensor(out=e_e[:, 512 * n:512 * (n + 1)], in0=pm[:, 0:512],
                                        in1=bout1[:][:, 512 * n:512 * (n + 1)], op=ALU.add)
                    e.tensor_tensor(out=resid[:, t, 0:512], in0=e_e[:, 0:512],
                                    in1=resid[:, t, 0:512], op=ALU.add)
                    i = e.tensor_tensor(out=resid[:, t, 512:1024], in0=e_e[:, 512:1024],
                                        in1=resid[:, t, 512:1024], op=ALU.add)
                    ctx.inc(i, V4)

    emit_l1attn()

    if _KMAX < 11:
        emit_store(ctx, nc, dout, resid, ST)
        stack.close()
        return nc

    ctx.free(["p1s", "p1v", "p1t",
              "f0ttr", "f0vev", "f0tmm", "f0vep", "f0sap",
              "p3w", "p3t", "p3v",
              "a1sap", "a1vre", "a1sex", "a1ttr", "a1vev", "a1tqk", "a1tpf",
              "a1tkv", "a1vqf", "a1vkv",
              "p5t", "p5v", "p5w", "p5g", "p5gc",
              "a2t", "a2s", "a2v", "a2e",
              "f1ttr", "f1vev", "f1tmm", "f1vep", "f1sap",
              "p8w", "p8s", "p8d", "p8v", "p8gz", "p8g2", "p8g3", "p8g",
              "p8t", "p8mxd", "p8mxc", "p8wgc",
              "f2wgu", "f2wdq", "f2tgu", "f2tdn", "f2sap", "f2txq", "f2sev",
              "f2ssg", "f2vhd", "f2thq", "f2acc"])
    emit_l1ffn("f3")

    with nc.Block() as b:
        @b.vector
        def _(e):
            e.tensor_copy(tsc1[:, 0:1], delta)
            e.tensor_copy(tsc1[:, 1:2], atot)
            e.tensor_copy(tsc1[:, 2:3], nhalfalpha[:])
            e.tensor_copy(tsc1[:, 3:4], lnM[:, 0:1])
            e.tensor_copy(tsc1[:, 4:5], lnQ[:, 0:1])
            e.tensor_copy(tsc1[:, 5:6], lnR[:, 0:1])
            e.tensor_copy(tsc1[:, 6:7], lnB[:, 0:1])
            e.tensor_copy(tsc1[:, 16:16 + D // 2], xln_bf[:, 0:D // 2])
    with nc.Block() as b:
        @b.gpsimd
        def _(e):
            ctx.inc(e.dma_start(out=ddbg[:, 0:D], in_=tsc1), ST, 16)
            ctx.inc(e.dma_start(out=ddbg[:, D:2 * D], in_=ternT[:, 0, :]), ST, 16)
            ctx.inc(e.dma_start(out=ddbg[:, 2 * D:3 * D], in_=terng[:, 0, :]), ST, 16)
            ctx.inc(e.dma_start(out=ddbg[:, 3 * D:4 * D], in_=xfm[:]), ST, 16)

    with nc.Block() as b:
        @b.gpsimd
        def _(e):
            ctx.inc(e.dma_start(out=dout.rearrange("(t p) d -> p t d", p=128), in_=resid[:]), ST, 16)
            e.wait_ge(ctx.sems[ST], ctx.cnt[ST])

    stack.close()
    return nc


def _prep_inputs(inputs):
    x = np.ascontiguousarray(np.asarray(inputs["L0_attn_x"], np.float32).reshape(B * T, D))
    iden = np.eye(128, dtype=ml_dtypes.bfloat16)
    bias0_b = np.ascontiguousarray(np.broadcast_to(np.asarray(inputs["L0_ffn_bias"], np.float32), (128, D)))
    sc = np.zeros((128, 8), np.float32)
    sc[:, 0] = np.asarray(inputs["L0_ffn_alpha"], np.float32)[0]
    sc[:, 1] = np.asarray(inputs["L1_ffn_s_in"], np.float32)[0]
    sc[:, 2] = np.asarray(inputs["L1_ffn_s_in_down"], np.float32)[0]
    sc[:, 3] = EPS_LN
    sc[:, 4] = 1.0
    sc[:, 5] = C_ONE
    sc[:, 6] = -C_ONE
    w0 = np.ascontiguousarray(np.asarray(inputs["L0_ffn_weight"], np.float32))
    wqkv0 = np.ascontiguousarray(np.asarray(inputs["L0_attn_w_qkv"], np.float32))
    wout0 = np.ascontiguousarray(np.asarray(inputs["L0_attn_w_out"], np.float32))
    gate1 = np.ascontiguousarray(np.asarray(inputs["L1_ffn_gate_w"], np.float32))
    up1 = np.ascontiguousarray(np.asarray(inputs["L1_ffn_up_w"], np.float32))
    down1 = np.ascontiguousarray(np.asarray(inputs["L1_ffn_down_w"], np.float32))
    wqkv1 = np.ascontiguousarray(np.asarray(inputs["L1_attn_w_qkv"], np.float32))
    proj1 = np.ascontiguousarray(np.asarray(inputs["L1_attn_proj"], np.float32))
    wout1 = np.ascontiguousarray(np.asarray(inputs["L1_attn_w_out"], np.float32))
    bout1_b = np.ascontiguousarray(
        np.broadcast_to(np.asarray(inputs["L1_attn_b_out"], np.float32), (128, D)))
    _n = np.arange(128)[:, None]
    _m = np.arange(128)[None, :]
    mm_sh = ((_n - _m >= 0) & (_n - _m <= 6)).astype(np.float32)
    _ne = np.arange(6)[:, None]
    me_std = (np.arange(128)[None, :] >= _ne + 122).astype(np.float32)
    in_maps = []
    for c in range(NCORE):
        pos = (TPC * (c % 4) + np.arange(TPC).reshape(NT, 128)).astype(np.float32)
        inv_freq = (np.float32(1.0) /
                    (np.float32(10000.0) ** (np.arange(0, HD0, 2, dtype=np.float32) / np.float32(HD0))))
        freqs = pos[:, :, None] * inv_freq[None, None, :]
        freqs = freqs.astype(np.float32)
        ckv = np.cos(freqs).astype(np.float32).transpose(1, 0, 2).reshape(128, NT * 32)
        skv = np.sin(freqs).astype(np.float32).transpose(1, 0, 2).reshape(128, NT * 32)
        im = {
            "xs": np.ascontiguousarray(x[c * TPC:(c + 1) * TPC]),
            "L0_ffn_weight": w0,
            "L0_attn_w_qkv": wqkv0,
            "L0_attn_w_out": wout0,
            "gate_strip": np.ascontiguousarray(gate1[512 * c:512 * (c + 1)]),
            "up_strip": np.ascontiguousarray(up1[512 * c:512 * (c + 1)]),
            "down_strip": np.ascontiguousarray(down1[:, 512 * c:512 * (c + 1)]),
            "scalars_c": sc,
            "bias0_b": bias0_b,
            "iden": iden,
            "ones128": np.ones((128, 128), np.float32),
            "rope_cos": np.ascontiguousarray(ckv),
            "rope_sin": np.ascontiguousarray(skv),
        }
        mm_t0 = mm_sh.copy()
        if c % 4 == 0:
            mm_t0[0:3, :] = 0.0
        me3 = np.concatenate([me_std, me_std], axis=1)
        if c % 4 == 3:
            me3[3:6, 128:256] = 0.0
        wsl = np.zeros((128, 8), ml_dtypes.bfloat16)
        if c % 4 != 0:
            wsl[:, c - 1] = 1.0
        wsr = np.zeros((128, 8), ml_dtypes.bfloat16)
        if c % 4 != 3:
            wsr[:, c + 1] = 1.0
        im.update({
            "L1_attn_w_qkv": wqkv1,
            "L1_attn_proj": proj1,
            "L1_attn_w_out": wout1,
            "bout1_b": bout1_b,
            "mm_sh": mm_sh,
            "mm_t0": np.ascontiguousarray(mm_t0),
            "me3": np.ascontiguousarray(me3),
            "wsel_l": wsl,
            "wsel_r": wsr,
        })
        in_maps.append(im)
    return in_maps


def kernel(**inputs):
    in_maps = _prep_inputs(inputs)
    nc = build_nc(0)
    res = run_bass_kernel_spmd(nc, in_maps, core_ids=list(range(NCORE)))
    out = np.concatenate([res.results[c]["out"] for c in range(NCORE)], axis=0)
    return out.reshape(B, T, D)
